# version 14
# baseline (speedup 1.0000x reference)
import numpy as np
import ml_dtypes
import contextlib
import concourse.bass as bass
import concourse.mybir as mybir
from concourse.bass_utils import run_bass_kernel_spmd

F32 = mybir.dt.float32
BF16 = mybir.dt.bfloat16
F16 = mybir.dt.float16
I32 = mybir.dt.int32
U8 = mybir.dt.uint8
AF = mybir.ActivationFunctionType
ALU = mybir.AluOpType
AXX = mybir.AxisListType.X

S = 4096
E = 2048
NH = 16
NI = 5632
TB = 512
NTT = 4
NEC = 16
NIC = 44
INV_SQRT_HD = 1.0 / float(np.sqrt(128.0))
LN_EPS = 1e-5
RMS_EPS = 1e-6

C_ONE, C_EXPSC, C_M8, C_M10, C_SGS, C_SGB, C_ZERO, C_MIN1 = range(8)
CONSTS = [1.0, INV_SQRT_HD, -8.0, -10.0, -1.0 / 16.0, 7.5 / 16.0, 0.0,
          -1.0, 0, 0, 0, 0, 0, 0, 0, 0]

ENGS = ("sync", "gpsimd", "vector", "scalar", "tensor")


class Buf:
    __slots__ = ("name", "w", "readers", "w_small")

    def __init__(self, name):
        self.name = name
        self.w = None
        self.w_small = False
        self.readers = []


class Sem:
    __slots__ = ("name", "h", "n", "lazy")

    def __init__(self, name, h):
        self.name = name
        self.h = h
        self.n = 0
        self.lazy = False


class Sched:
    def __init__(self, nc=None, st=None):
        self.ops = []
        self.sems = {}
        self.water = {e: {} for e in ENGS}
        self.bufs = {}
        self.n_instr = 0
        self.nc = nc
        self.st = st

    def gb(self, name):
        if name not in self.bufs:
            self.bufs[name] = Buf(name)
        return self.bufs[name]

    def sem_for(self, eng, dma):
        if dma:
            key = "d_" + dma
            if key not in self.sems:
                h = self.st.enter_context(self.nc.semaphore(key))
                self.sems[key] = Sem(key, h)
            return self.sems[key]
        key = {"vector": "eV", "scalar": "eA", "tensor": "eT",
               "gpsimd": "eP"}[eng]
        return self.sems[key]

    def op(self, eng, emit, r=(), w=(), dma=False, lazy=False, small=False,
           tag=""):
        sem = self.sem_for(eng, dma)
        amt = 16 if dma else 1
        reads = [self.gb(x) for x in r]
        writes = [self.gb(x) for x in w]
        deps = []
        for b in reads:
            if b.w is not None:
                deps.append((b.w[0], b.w[1], b.w_small))
        for b in writes:
            if b.w is not None:
                deps.append((b.w[0], b.w[1], False))
            deps.extend((s_, v_, False) for (s_, v_) in b.readers)
        deps = [(s, s.n if s.name.startswith("d_") else v, f)
                for (s, v, f) in deps]
        need = {}
        own = None if dma else sem
        for (s, v, force) in deps:
            if s is own and not force:
                continue
            if need.get(s, -1) < v:
                need[s] = v
        wm = self.water[eng]
        waits = []
        for s, v in sorted(need.items(), key=lambda kv: kv[0].name):
            if wm.get(s, -1) >= v:
                continue
            wm[s] = v
            waits.append((s, v))
        if lazy:
            assert not dma
            val = sem.n + 1
            sem.lazy = True
        else:
            sem.n += amt
            val = sem.n
            sem.lazy = False
        for b in writes:
            b.w = (sem, val)
            b.w_small = small
            b.readers = []
        for b in reads:
            b.readers.append((sem, val))
        self.ops.append((eng, waits, emit, None if lazy else sem, amt, tag))
        self.n_instr += 1 + len(waits)

    def claim(self, eng, emitnop, names, tag="claim"):
        self.op(eng, emitnop, r=(), w=names, tag=tag)

    def barrier(self):
        for s in self.sems.values():
            assert not s.lazy, f"lazy pending on {s.name} at barrier"
        cur = [(s, s.n) for s in self.sems.values() if s.n > 0]
        for e in ENGS:
            wm = self.water[e]
            todo = [(s, v) for (s, v) in cur if wm.get(s, -1) < v]
            for (s, v) in todo:
                wm[s] = v
            if todo:
                self.ops.append((e, todo, None, None, 0, "barrier"))
                self.n_instr += len(todo)

    def emit(self, block):
        def mk(name):
            def body(eng):
                for (e, waits, emit, sem, amt, tag) in self.ops:
                    if e != name:
                        continue
                    for (s, v) in waits:
                        eng.wait_ge(s.h, v)
                    if emit is None:
                        continue
                    ins = emit(eng)
                    if sem is not None:
                        ins.then_inc(sem.h, amt)
            return body
        block.sync(mk("sync"))
        block.gpsimd(mk("gpsimd"))
        block.vector(mk("vector"))
        block.scalar(mk("scalar"))
        block.tensor(mk("tensor"))


def build_program(nblk=8, taps=None, dbg_stop=None, rd=True):
    nc = bass.Bass("TRN2", target_bir_lowering=False,
                   detect_race_conditions=rd)
    st = contextlib.ExitStack()

    x_d = nc.dram_tensor("x", [S, E], F32, kind="ExternalInput").ap()
    wqkv_d = nc.dram_tensor("wqkv", [3 * E, E], F32, kind="ExternalInput").ap()
    wout_d = nc.dram_tensor("wout", [E, E], F32, kind="ExternalInput").ap()
    gup_d = nc.dram_tensor("gup", [2 * NI, E // 8], I32, kind="ExternalInput").ap()
    gus_d = nc.dram_tensor("gus", [16, 2 * NI], F16, kind="ExternalInput").ap()
    dnp_d = nc.dram_tensor("dnp", [E, NI // 8], I32, kind="ExternalInput").ap()
    dns_d = nc.dram_tensor("dns", [48, E], F16, kind="ExternalInput").ap()
    ident_d = nc.dram_tensor("identin", [128, 128], BF16, kind="ExternalInput").ap()
    onesc_d = nc.dram_tensor("onescin", [128, 1], BF16, kind="ExternalInput").ap()
    onesr_d = nc.dram_tensor("onesrin", [1, 128], BF16, kind="ExternalInput").ap()
    mask_d = nc.dram_tensor("maskin", [128, 2048], BF16, kind="ExternalInput").ap()
    cst_d = nc.dram_tensor("cstin", [128, 16], F32, kind="ExternalInput").ap()
    y_d = nc.dram_tensor("y", [S, E], F32, kind="ExternalOutput").ap()
    sqP_d = nc.dram_tensor("scr_qP", [4 * 12 * 128, 2048], BF16, kind="Internal").ap()
    soP_d = nc.dram_tensor("scr_oP", [4 * 4 * 128, 2048], BF16, kind="Internal").ap()
    sguP_d = nc.dram_tensor("scr_guP", [4 * 22 * 128, 2048], BF16, kind="Internal").ap()
    sdnP_d = nc.dram_tensor("scr_dnP", [11 * 4 * 128, 2048], BF16, kind="Internal").ap()

    tap_outs = []
    if taps:
        for (name, shape, dtype) in taps:
            tap_outs.append((name, nc.dram_tensor(name, shape, dtype,
                                                  kind="ExternalOutput").ap()))

    sb = lambda name, shape, dt: st.enter_context(nc.sbuf_tensor(name, shape, dt))
    A = sb("A", [128, 4 * E], F32)
    B = sb("B", [128, 4 * E], F32)
    xbf = sb("xbf", [128, E], BF16)
    xfm = sb("xfm", [128, NEC * TB], BF16)
    kf0 = sb("kf0", [128, NH * TB], BF16)
    kf1 = sb("kf1", [128, NH * TB], BF16)
    vt0 = sb("vt0", [128, NTT * E], BF16)
    vt1 = sb("vt1", [128, NTT * E], BF16)
    ar1 = sb("ar1", [128, NIC * TB], BF16)
    wb = sb("wb", [128, 8 * TB], BF16)
    mkp = sb("mkp", [128, 1024], BF16)
    mko = sb("mko", [128, 1024], BF16)
    gs = sb("gs", [128, TB], BF16)
    ident = sb("ident", [128, 128], BF16)
    onescol = sb("onescol", [128, 1], BF16)
    onesrow = sb("onesrow", [1, 128], BF16)
    cst = sb("cst", [128, 16], F32)
    stats = sb("stats", [128, 64], F32)

    qf = ar1[:, 0:NH * TB]
    es = ar1[:, NH * TB:NH * TB + 4 * TB]
    af = ar1[:, 20 * TB:36 * TB]
    recb = ar1[:, 36 * TB:38 * TB].bitcast(F32)
    rh = ar1[:, 38 * TB:39 * TB][0:1, 0:512]
    rl = ar1[:, 39 * TB:40 * TB][0:1, 0:512]
    srow = ar1[:, 40 * TB:42 * TB].bitcast(F32)[0:1, 0:512]
    h_fm = ar1
    wbF = ar1[:, 8192:16384].bitcast(F32)

    AR1_ATTN = ["qf", "esA", "es0", "es1", "es2", "es3", "af", "rec",
                "srow", "rhi", "rlo"]

    pq = st.enter_context(nc.psum_tensor("pq", [128, 2048], F32))
    pb = [pq[:, 512 * i:512 * (i + 1)] for i in range(4)] + \
         [st.enter_context(nc.psum_tensor(f"pb{i}", [128, 512], F32))
          for i in range(4, 8)]
    pbh = {k: pb[k].bitcast(BF16) for k in (4, 5, 6, 7)}

    sch = Sched(nc, st)
    for sname in ("eV", "eA", "eT", "eP"):
        h = st.enter_context(nc.semaphore(sname))
        sch.sems[sname] = Sem(sname, h)

    OP = sch.op

    def cc(i):
        return cst[:, i:i + 1]

    nop = lambda e: e.engine_nop()

    OP("sync", lambda e: e.dma_start(out=ident[:], in_=ident_d), w=["ident"], dma="c")
    OP("sync", lambda e: e.dma_start(out=onescol[:], in_=onesc_d), w=["ones"], dma="c")
    OP("sync", lambda e: e.dma_start(out=onesrow[:], in_=onesr_d), w=["ones"], dma="c")
    OP("sync", lambda e: e.dma_start(out=mkp[:], in_=mask_d[:, 0:1024]), w=["mkp"], dma="c")
    OP("sync", lambda e: e.dma_start(out=mko[:], in_=mask_d[:, 1024:2048]), w=["mko"], dma="c")
    OP("sync", lambda e: e.dma_start(out=cst[:], in_=cst_d), w=["cst"], dma="c")

    stgA = xfm
    stgA5 = stgA.rearrange("p (g l o f) -> p g l o f", g=4, l=4, o=4)

    def tA_stage(dense, dense_tag, osub, tag):
        for g3 in range(4):
            k = 6 + g3 % 2
            for ecl in range(4):
                ec = 4 * g3 + ecl
                OP("tensor", lambda e, ec=ec, ecl=ecl, k=k, dense=dense:
                   e.transpose(pbh[k][:, 128 * ecl:128 * ecl + 128],
                               dense[:, 128 * ec:128 * ec + 128], ident[:]),
                   r=[dense_tag, "ident"], w=[f"ps{k}"], lazy=(ecl < 3),
                   tag=tag + "T")
            OP("scalar", lambda e, g3=g3, osub=osub, k=k:
               e.copy(stgA5[:, g3, :, osub, :],
                      pbh[k][:, 0:512].rearrange("p (l f) -> p l f", l=4)),
               r=[f"ps{k}"], w=[f"sa{g3}_{osub}"], tag=tag + "Tev")

    def tA_flush(dstP, nf, fg, tag):
        for g3 in range(4):
            rb = (g3 * nf + fg) * 128
            OP("scalar", lambda e, g3=g3, rb=rb, dstP=dstP: e.dma_start(
                out=dstP[rb:rb + 128, :],
                in_=stgA[:, 2048 * g3:2048 * g3 + 2048]),
                r=[f"sa{g3}_0", f"sa{g3}_1", f"sa{g3}_2", f"sa{g3}_3"],
                dma=f"fa{g3}", tag=tag + "st")

    w32 = (kf0.bitcast(F32)[:, 0:E], kf1.bitcast(F32)[:, 0:E])
    dstg = (vt0[:, 0:E], vt1[:, 0:E])

    emitA = []

    def mk_cast_unit(src_d, dstP, nf, c):
        sl = c % 2
        wsrc, wdst = w32[sl], dstg[sl]
        ldb, stb = f"wld{sl}", f"wst{sl}"

        def unit():
            OP("sync", lambda e, wsrc=wsrc, c=c, src_d=src_d: e.dma_start(
                out=wsrc[:, :], in_=src_d[128 * c:128 * c + 128, :]),
                w=[ldb], dma=f"wl{sl}", tag="wc_ld")
            OP("vector", lambda e, wsrc=wsrc, wdst=wdst:
               e.tensor_copy(wdst[:, :], wsrc[:, :]),
               r=[ldb], w=[stb], tag="wc_cp")
            tA_stage(wdst, stb, c % 4, "wc")
            if c % 4 == 3:
                tA_flush(dstP, nf, c // 4, "wc")
        return unit

    for c in range(3 * E // 128):
        emitA.append(mk_cast_unit(wqkv_d, sqP_d, 12, c))
    for c in range(E // 128):
        emitA.append(mk_cast_unit(wout_d, soP_d, 4, c))

    U16 = mybir.dt.uint16
    DQ = []
    for p in range(2):
        b0 = 4608 * p
        DQ.append(dict(
            by=ar1[:, b0:b0 + 512].bitcast(U16),
            t1=ar1[:, b0 + 512:b0 + 1024].bitcast(U16),
            t2=ar1[:, b0 + 1024:b0 + 1536].bitcast(U16),
            W=ar1[:, b0 + 1536:b0 + 2560],
            G=ar1[:, b0 + 2560:b0 + 3584],
            C=ar1[:, b0 + 3584:b0 + 4608],
        ))
    guld = ar1[:, 9216:10240].bitcast(I32)
    dnld = ar1[:, 10240:11648].bitcast(I32)
    stgG = ar1[:, 11648:19840]
    stgG5 = stgG.rearrange("p (g l o f) -> p g l o f", g=4, l=4, o=4)
    stgD = ar1[:, 19840:21888]
    sc_gu = [ar1[:, 21888 + 32 * p:21920 + 32 * p].bitcast(F32) for p in range(2)]
    sc_guh = [ar1[:, 21952 + 16 * p:21968 + 16 * p].bitcast(F16) for p in range(2)]
    sc_dn = [ar1[:, 21984 + 88 * p:22072 + 88 * p].bitcast(F32) for p in range(2)]
    sc_dnh = [ar1[:, 22160 + 48 * p:22208 + 48 * p].bitcast(F16) for p in range(2)]

    tcount = [0]

    def dq_tile(words_ap, nwords, scales_ap, sctag, ldtag, store_fn):
        p = tcount[0] % 2
        tcount[0] += 1
        d = DQ[p]
        nb = 4 * nwords
        ne = 8 * nwords
        by = d["by"][:, 0:nb]
        t1 = d["t1"][:, 0:nb]
        t2 = d["t2"][:, 0:nb]
        Wu = d["W"].bitcast(U16)
        Gu = d["G"].bitcast(U16)
        Wb = d["W"][:, 0:ne]
        Gb = d["G"][:, 0:ne]
        Cb = d["C"][:, 0:ne]
        nby, nt1, nt2 = f"q{p}by", f"q{p}t1", f"q{p}t2"
        nW, nG, nC = f"q{p}W", f"q{p}G", f"q{p}C"
        OP("vector", lambda e: e.tensor_copy(by[:, :], words_ap.bitcast(U8)),
           r=[ldtag], w=[nby], tag="dq_by")
        for h in range(2):
            Wh = Wu[:, h * nb:h * nb + nb]
            Gh = Gu[:, h * nb:h * nb + nb]
            if h == 0:
                OP("vector", lambda e: e.tensor_scalar(
                    t1[:, :], by[:, :], 15, 6, ALU.bitwise_and,
                    ALU.logical_shift_left), r=[nby], w=[nt1], tag="dq_t1")
                OP("vector", lambda e: e.tensor_scalar(
                    t2[:, :], by[:, :], 8, 6, ALU.bitwise_and,
                    ALU.logical_shift_left), r=[nby], w=[nt2], tag="dq_t2")
            else:
                OP("vector", lambda e: e.tensor_scalar(
                    t1[:, :], by[:, :], 4, 6, ALU.logical_shift_right,
                    ALU.logical_shift_left), r=[nby], w=[nt1], tag="dq_t1")
                OP("vector", lambda e: e.tensor_scalar(
                    t2[:, :], by[:, :], 128, 2, ALU.bitwise_and,
                    ALU.logical_shift_left), r=[nby], w=[nt2], tag="dq_t2")
            OP("vector", lambda e, Wh=Wh: e.tensor_tensor(
                Wh[:, :], t1[:, :], t2[:, :], ALU.subtract),
               r=[nt1, nt2], w=[nW], tag="dq_sub")
            OP("vector", lambda e, Wh=Wh: e.tensor_scalar(
                Wh[:, :], Wh[:, :], 16128, None, ALU.add),
               r=[nW], w=[nW], tag="dq_add")
            OP("vector", lambda e, Gh=Gh: e.tensor_scalar(
                Gh[:, :], t2[:, :], 6, 16256, ALU.logical_shift_left,
                ALU.bitwise_or), r=[nt2], w=[nG], tag="dq_g")
        OP("scalar", lambda e: e.activation(
            Cb[:, :], Wb[:, :], AF.Relu, bias=cc(C_ONE), scale=cc(C_MIN1)),
           r=[nW, "cst"], w=[nC], tag="dq_relu")
        OP("vector", lambda e: e.tensor_tensor(
            Wb[:, :], Wb[:, :], Cb[:, :], ALU.subtract),
           r=[nW, nC], w=[nW], tag="dq_mag")
        OP("vector", lambda e: e.tensor_tensor(
            Wb[:, :], Wb[:, :], Gb[:, :], ALU.mult),
           r=[nW, nG], w=[nW], tag="dq_val")
        ng = ne // 128
        W3 = Wb.rearrange("p (hh c) -> p hh c", hh=2)
        for g in range(ng):
            OP("vector", lambda e, g=g: e.tensor_scalar(
                W3[:, :, 64 * g:64 * (g + 1)], W3[:, :, 64 * g:64 * (g + 1)],
                scales_ap[:, g:g + 1], None, ALU.mult),
                r=[sctag], w=[nW], lazy=(g < ng - 1), tag="dq_sc")
        store_fn(Wb, nW, p)

    def mk_gu_unit(c):
        sl = c % 2
        wsl = guld[:, 256 * sl:256 * sl + 256]

        def unit():
            OP("sync", lambda e, c=c, wsl=wsl: e.dma_start(
                out=wsl[:, :], in_=gup_d[128 * c:128 * c + 128, :]),
                w=[f"gld{sl}"], dma=f"gw{sl}", tag="gu_ld")
            OP("sync", lambda e, c=c: e.dma_start(
                out=sc_guh[sl][:, :], in_=gus_d[:, 128 * c:128 * c + 128],
                transpose=True), w=[f"gsch{sl}"], dma=f"gs{sl}", tag="gu_scld")
            OP("vector", lambda e: e.tensor_copy(sc_gu[sl][:, :],
                                                 sc_guh[sl][:, :]),
               r=[f"gsch{sl}"], w=[f"gscf{sl}"], small=True, tag="gu_scvt")
            for half in range(2):
                wv = wsl[:, 128 * half:128 * (half + 1)]

                def store_gu(dense, dtag, p, c=c, half=half):
                    osub = c % 4
                    for g3 in range(2 * half, 2 * half + 2):
                        k = 4 + g3 % 2
                        for ecl in range(4):
                            el = (g3 - 2 * half) * 4 + ecl
                            OP("tensor", lambda e, el=el, ecl=ecl, k=k,
                               dense=dense: e.transpose(
                                   pbh[k][:, 128 * ecl:128 * ecl + 128],
                                   dense[:, 128 * el:128 * el + 128],
                                   ident[:]),
                               r=[dtag, "ident"], w=[f"ps{k}"],
                               lazy=(ecl < 3), tag="guT")
                        OP("scalar", lambda e, g3=g3, osub=osub, k=k: e.copy(
                            stgG5[:, g3, :, osub, :],
                            pbh[k][:, 0:512].rearrange("p (l f) -> p l f",
                                                       l=4)),
                            r=[f"ps{k}"], w=[f"sg{g3}_{osub}"], tag="guTev")
                    if c % 4 == 3 and half == 1:
                        colidx = c // 4
                        for g3 in range(4):
                            rb = (g3 * 22 + colidx) * 128
                            OP("scalar", lambda e, g3=g3, rb=rb: e.dma_start(
                                out=sguP_d[rb:rb + 128, :],
                                in_=stgG[:, 2048 * g3:2048 * g3 + 2048]),
                                r=[f"sg{g3}_0", f"sg{g3}_1", f"sg{g3}_2",
                                   f"sg{g3}_3"], dma=f"fg{g3}", tag="gu_st")

                dq_tile(wv, 128, sc_gu[sl][:, 8 * half:8 * half + 8],
                        f"gscf{sl}", f"gld{sl}", store_gu)
        return unit

    emitG = [mk_gu_unit(c) for c in range(2 * NI // 128)]

    def mk_dn_unit(c):
        sl = c % 2

        def unit():
            OP("sync", lambda e, c=c: e.dma_start(
                out=dnld[:, :], in_=dnp_d[128 * c:128 * c + 128, :]),
                w=["dld"], dma="gw0", tag="dn_ld")
            for tt in range(3):
                OP("sync", lambda e, c=c, tt=tt: e.dma_start(
                    out=sc_dnh[sl][:, 16 * tt:16 * tt + 16],
                    in_=dns_d[16 * tt:16 * tt + 16, 128 * c:128 * c + 128],
                    transpose=True), w=[f"dsch{sl}"], dma=f"gs{sl}",
                    tag="dn_scld")
            OP("vector", lambda e: e.tensor_copy(sc_dn[sl][:, :],
                                                 sc_dnh[sl][:, 0:44]),
               r=[f"dsch{sl}"], w=[f"dscf{sl}"], small=True, tag="dn_scvt")
            for bq in range(6):
                nwords = 128 if bq < 5 else 64
                wv = dnld[:, 128 * bq:128 * bq + nwords]

                def store_dn(dense, dtag, p, c=c, bq=bq, nwords=nwords):
                    for g in range(nwords // 64):
                        k = 4 + (bq * 2 + g) % 2
                        for ecl in range(4):
                            el = 4 * g + ecl
                            OP("tensor", lambda e, el=el, ecl=ecl, k=k,
                               dense=dense: e.transpose(
                                   pbh[k][:, 128 * ecl:128 * ecl + 128],
                                   dense[:, 128 * el:128 * el + 128],
                                   ident[:]),
                               r=[dtag, "ident"], w=[f"ps{k}"],
                               lazy=(ecl < 3), tag="dnT")
                        OP("scalar", lambda e, p=p, g=g, k=k: e.copy(
                            stgD[:, 1024 * p + 512 * g:1024 * p + 512 * g + 512],
                            pbh[k][:, 0:512]),
                            r=[f"ps{k}"], w=[f"sd{p}_{g}"], tag="dnTev")
                        for i in range(4):
                            icg = 8 * bq + 4 * g + i
                            rb = ((icg // 4) * 4 + c // 4) * 128
                            co = 512 * (icg % 4) + 128 * (c % 4)
                            OP("scalar", lambda e, rb=rb, co=co, p=p, g=g, i=i:
                               e.dma_start(
                                   out=sdnP_d[rb:rb + 128, co:co + 128],
                                   in_=stgD[:, 1024 * p + 512 * g + 128 * i:
                                            1024 * p + 512 * g + 128 * i + 128]),
                               r=[f"sd{p}_{g}"], dma=f"fd{icg % 4}",
                               tag="dn_st")

                dq_tile(wv, nwords, sc_dn[sl][:, 8 * bq:8 * bq + nwords // 16],
                        f"dscf{sl}", "dld", store_dn)
        return unit

    emitD = [mk_dn_unit(c) for c in range(E // 128)]

    na, ngu = len(emitA), len(emitG)
    done_a = 0
    for i, gfn in enumerate(emitG):
        gfn()
        want = ((i + 1) * na) // ngu
        while done_a < want:
            emitA[done_a]()
            done_a += 1
    while done_a < na:
        emitA[done_a]()
        done_a += 1
    for dfn in emitD:
        dfn()

    sch.barrier()

    wcount = [0]

    def stream_half(srcP, nf, kq, colidx, j):
        i = wcount[0] % 4
        wcount[0] += 1
        half = wb[:, 1024 * i:1024 * i + 1024]
        rb = (kq * nf + colidx) * 128
        eng = "sync" if i % 2 == 0 else "gpsimd"
        OP(eng, lambda e, half=half, rb=rb, j=j, srcP=srcP: e.dma_start(
            out=half, in_=srcP[rb:rb + 128, 1024 * j:1024 * j + 1024]),
            w=[f"wq{i}"], dma=f"wq{i}", tag="wstream")
        return half, f"wq{i}"

    WBALL = ["wq0", "wq1", "wq2", "wq3"]

    def transpose_into_xfm(r, src_bf):
        x5 = src_bf.rearrange("p (b J t h) -> p b h J t", b=2, J=128, t=4, h=2)
        for ec in range(NEC):
            b_, hh, q = ec // 8, (ec % 8) // 4, ec % 4
            k = 6 + (ec % 2)
            OP("tensor", lambda e, b_=b_, hh=hh, q=q, k=k, x5=x5:
               e.transpose(pbh[k][:, 0:128],
                           x5[:, b_, hh, 32 * q:32 * (q + 1), :], ident[:]),
               r=["xbf", "ident"], w=[f"ps{k}"], tag="xT")
            OP("scalar", lambda e, ec=ec, r=r, k=k:
               e.copy(xfm[:, TB * ec + 128 * r:TB * ec + 128 * r + 128],
                      pbh[k][:, 0:128]),
               r=[f"ps{k}"], w=["xfm"], tag="xTev")

    kf = (kf0, kf1)
    vt = (vt0, vt1)

    for blk in range(nblk):
        par = blk % 2
        kfC, kfP = kf[par], kf[1 - par]
        vtC, vtP = vt[par], vt[1 - par]
        kfCt, kfPt = f"kf{par}", f"kf{1 - par}"
        vtCt, vtPt = f"vt{par}", f"vt{1 - par}"
        tb0 = blk * TB

        sch.claim("vector", nop, ["h"] + AR1_ATTN, tag="claimA")

        for r in range(NTT):
            Br = B[:, E * r:E * (r + 1)]
            OP("scalar", lambda e, r=r, Br=Br, tb0=tb0: e.dma_start(
                out=Br[:, :], in_=x_d[tb0 + 128 * r:tb0 + 128 * r + 128, :]),
                w=[f"B{r}"], dma=f"xl{r}", tag="xload")
            OP("vector", lambda e, Br=Br: e.tensor_copy(xbf[:], Br[:, :]),
               r=[f"B{r}"], w=["xbf"], tag="xcast")
            transpose_into_xfm(r, xbf)

        if dbg_stop == "p1" and blk == nblk - 1:
            break
        for fg in range(12):
            bks = (0, 1, 2, 3) if fg % 2 == 0 else (4, 5, 6, 7)
            for kq in range(4):
                for j in range(2):
                    qslot, stag = stream_half(sqP_d, 12, kq, fg, j)
                    for kcl in range(2 * j, 2 * j + 2):
                        kc = 4 * kq + kcl
                        slot = qslot[:, 512 * (kcl - 2 * j):512 * (kcl - 2 * j) + 512]
                        for m in range(4):
                            first, last = kc == 0, kc == NEC - 1
                            bk = bks[m]
                            if fg < 8:
                                OP("tensor", lambda e, slot=slot, kc=kc, m=m, bk=bk,
                                   first=first, last=last: e.matmul(
                                       pb[bk][:, :], slot[:, 128 * m:128 * (m + 1)],
                                       xfm[:, TB * kc:TB * (kc + 1)],
                                       start=first, stop=last),
                                   r=[stag, "xfm"], w=[f"ps{bk}"], lazy=(m < 3),
                                   tag="qk_mm")
                            else:
                                OP("tensor", lambda e, slot=slot, kc=kc, m=m, bk=bk,
                                   first=first, last=last: e.matmul(
                                       pb[bk][:, :],
                                       xfm[:, TB * kc + 128 * m:TB * kc + 128 * (m + 1)],
                                       slot[:, :], start=first, stop=last),
                                   r=[stag, "xfm"], w=[f"ps{bk}"], lazy=(m < 3),
                                   tag="v_mm")
            for m in range(4):
                bk = bks[m]
                if fg < 4:
                    hch = 4 * fg + m
                    OP("scalar", lambda e, hch=hch, bk=bk:
                       e.copy(qf[:, TB * hch:TB * (hch + 1)], pb[bk][:, :]),
                       r=[f"ps{bk}"], w=["qf"], tag="q_ev")
                elif fg < 8:
                    hch = 4 * (fg - 4) + m
                    OP("scalar", lambda e, hch=hch, bk=bk, kfC=kfC:
                       e.copy(kfC[:, TB * hch:TB * (hch + 1)], pb[bk][:, :]),
                       r=[f"ps{bk}"], w=[kfCt], tag="k_ev")
                else:
                    vg = fg - 8
                    OP("scalar", lambda e, vg=vg, m=m, bk=bk, vtC=vtC:
                       e.copy(vtC[:, E * m + 512 * vg:E * m + 512 * (vg + 1)],
                              pb[bk][:, :]),
                       r=[f"ps{bk}"], w=[vtCt], tag="v_ev")

        if dbg_stop == "p2" and blk == nblk - 1:
            break
        c_lo = 4 if blk == 0 else 0
        for hh in range(NH):
            pvb = 4 + (hh % 2)
            for half in range(c_lo // 4, 2):
                for cl in range(4):
                    c = 4 * half + cl
                    if c < 4:
                        ksrc, ktag = kfP, kfPt
                        kcol = TB * hh + 128 * c
                    else:
                        ksrc, ktag = kfC, kfCt
                        kcol = TB * hh + 128 * (c - 4)
                    ql = max(0, 128 * (c - 4))
                    qh = min(512, 128 * (c + 1))
                    OP("tensor", lambda e, ksrc=ksrc, kcol=kcol, cl=cl, hh=hh,
                       ql=ql, qh=qh:
                       e.matmul(pb[cl][:, ql:qh], ksrc[:, kcol:kcol + 128],
                                qf[:, TB * hh + ql:TB * hh + qh], start=True,
                                stop=True),
                       r=[ktag, "qf"], w=[f"ps{cl}"], lazy=(cl < 3),
                       tag="score")
                OP("scalar", lambda e: e.activation(
                    es[:, :], pq[:, :], AF.Exp, bias=cc(C_ZERO),
                    scale=cc(C_EXPSC)),
                    r=["ps0", "ps1", "ps2", "ps3", "cst"],
                    w=["esA", "es0", "es1", "es2", "es3"], tag="exp")
                for cl in range(4):
                    c = 4 * half + cl
                    if c < 4:
                        msrc, mt = mkp, "mkp"
                        mcol = 512 - 128 * c
                        vsrc, vtag = vtP, vtPt
                        vcol = E * c + 128 * hh
                    else:
                        msrc, mt = mko, "mko"
                        mcol = 512 - 128 * (c - 4)
                        vsrc, vtag = vtC, vtCt
                        vcol = E * (c - 4) + 128 * hh
                    OP("vector", lambda e, cl=cl, msrc=msrc, mcol=mcol:
                       e.tensor_tensor(es[:, 512 * cl:512 * cl + 512],
                                       es[:, 512 * cl:512 * cl + 512],
                                       msrc[:, mcol:mcol + 512], ALU.mult),
                       r=["esA", mt], w=[f"es{cl}"], tag="mask")
                    OP("tensor", lambda e, cl=cl, c=c, c_lo=c_lo: e.matmul(
                        pb[6][0:1, :], onescol[:, 0:1],
                        es[:, 512 * cl:512 * cl + 512],
                        start=(c == c_lo), stop=(c == 7)),
                        r=[f"es{cl}", "ones"], w=["ps6"], lazy=(c < 7),
                        tag="summ")
                    OP("tensor", lambda e, vsrc=vsrc, vcol=vcol, cl=cl, c=c,
                       c_lo=c_lo, pvb=pvb:
                       e.matmul(pb[pvb][:, :], vsrc[:, vcol:vcol + 128],
                                es[:, 512 * cl:512 * cl + 512],
                                start=(c == c_lo), stop=(c == 7)),
                       r=[vtag, f"es{cl}"], w=[f"ps{pvb}"], lazy=(c < 7),
                       tag="pv")
            OP("scalar", lambda e: e.copy(srow[:, :], pb[6][0:1, :]),
               r=["ps6"], w=["srow"], small=True, tag="sum_ev")
            OP("vector", lambda e: e.tensor_copy(rh[:, :], srow[:, :]),
               r=["srow"], w=["rhi"], small=True, tag="rhi")
            OP("vector", lambda e: e.scalar_tensor_tensor(
                rl[:, :], rh[:, :], -1.0, srow[:, :], ALU.mult, ALU.add),
                r=["rhi", "srow"], w=["rlo"], small=True, tag="rlo")
            OP("tensor", lambda e: e.matmul(pb[7][:, :], onesrow[:, :],
                                            rh[:, :], start=True, stop=False),
               r=["rhi", "ones"], w=["ps7"], lazy=True, tag="bch")
            OP("tensor", lambda e: e.matmul(pb[7][:, :], onesrow[:, :],
                                            rl[:, :], start=False, stop=True),
               r=["rlo", "ones"], w=["ps7"], tag="bcl")
            OP("vector", lambda e: e.reciprocal(recb[:, :], pb[7][:, :]),
               r=["ps7"], w=["rec"], tag="recip")
            OP("vector", lambda e, hh=hh, pvb=pvb: e.scalar_tensor_tensor(
                af[:, TB * hh:TB * (hh + 1)], pb[pvb][:, :], 1.0, recb[:, :],
                ALU.bypass, ALU.mult),
                r=[f"ps{pvb}", "rec"], w=["af"], tag="attn_ev")

        if dbg_stop == "p3" and blk == nblk - 1:
            break
        for og in range(4):
            bks = (0, 1, 2, 3) if og % 2 == 0 else (4, 5, 6, 7)
            for kq in range(4):
                for j in range(2):
                    qslot, stag = stream_half(soP_d, 4, kq, og, j)
                    for kcl in range(2 * j, 2 * j + 2):
                        kc = 4 * kq + kcl
                        slot = qslot[:, 512 * (kcl - 2 * j):512 * (kcl - 2 * j) + 512]
                        for m in range(4):
                            bk = bks[m]
                            OP("tensor", lambda e, slot=slot, kc=kc, m=m, bk=bk:
                               e.matmul(pb[bk][:, :],
                                        af[:, TB * kc + 128 * m:TB * kc + 128 * (m + 1)],
                                        slot[:, :], start=(kc == 0),
                                        stop=(kc == NEC - 1)),
                               r=[stag, "af"], w=[f"ps{bk}"], lazy=(m < 3),
                               tag="pj_mm")
            for r in range(4):
                bk = bks[r]
                OP("vector", lambda e, r=r, og=og, bk=bk: e.scalar_tensor_tensor(
                    A[:, E * r + 512 * og:E * r + 512 * (og + 1)], pb[bk][:, :],
                    1.0, B[:, E * r + 512 * og:E * r + 512 * (og + 1)],
                    ALU.bypass, ALU.add),
                    r=[f"ps{bk}", f"B{r}"], w=[f"A{r}"], tag="resid1")
        if dbg_stop == "p4a" and blk == nblk - 1:
            break
        for r in range(NTT):
            Ar = A[:, E * r:E * (r + 1)]
            Br = B[:, E * r:E * (r + 1)]
            sc = 6 * r

            def s1(i, r=r, sc=sc):
                return stats[:, sc + i:sc + i + 1], f"st{i}_{r}"
            OP("vector", lambda e, Ar=Ar, sc=sc: e.reduce_sum(
                stats[:, sc:sc + 1], Ar[:, :], axis=AXX),
               r=[f"A{r}"], w=[s1(0)[1]], small=True, tag="l1sum")
            OP("vector", lambda e, sc=sc: e.tensor_scalar(
                stats[:, sc + 1:sc + 2], stats[:, sc:sc + 1],
                1.0 / E, None, ALU.mult),
               r=[s1(0)[1]], w=[s1(1)[1]], small=True, tag="l1mean")
            OP("vector", lambda e, Ar=Ar, Br=Br, sc=sc: e.tensor_scalar(
                Br[:, :], Ar[:, :], stats[:, sc + 1:sc + 2], None, ALU.subtract),
                r=[f"A{r}", s1(1)[1]], w=[f"B{r}"], tag="l1ctr")
            OP("scalar", lambda e, Br=Br, sc=sc: e.activation(
                xbf[:], Br[:, :], AF.Square, bias=cc(C_ZERO), scale=cc(C_ONE),
                accum_out=stats[:, sc + 2:sc + 3]),
                r=[f"B{r}", "cst"], w=[s1(2)[1], "xbf"], tag="l1sq")
            OP("vector", lambda e, sc=sc: e.tensor_scalar(
                stats[:, sc + 3:sc + 4], stats[:, sc + 2:sc + 3], 1.0 / E,
                LN_EPS, ALU.mult, ALU.add),
               r=[s1(2)[1]], w=[s1(3)[1]], small=True, tag="l1var")
            OP("scalar", lambda e, sc=sc: e.activation(
                stats[:, sc + 4:sc + 5], stats[:, sc + 3:sc + 4], AF.Sqrt,
                bias=cc(C_ZERO), scale=cc(C_ONE)),
               r=[s1(3)[1], "cst"], w=[s1(4)[1]], small=True, tag="l1std")
            OP("vector", lambda e, sc=sc: e.reciprocal(
                stats[:, sc + 5:sc + 6], stats[:, sc + 4:sc + 5]),
               r=[s1(4)[1]], w=[s1(5)[1]], small=True, tag="l1rstd")
            OP("vector", lambda e, Ar=Ar, Br=Br, sc=sc: e.tensor_scalar(
                Ar[:, :], Br[:, :], stats[:, sc + 5:sc + 6], None, ALU.mult),
                r=[f"B{r}", s1(5)[1]], w=[f"A{r}"], tag="l1app")
            OP("vector", lambda e, Ar=Ar: e.tensor_copy(xbf[:], Ar[:, :]),
               r=[f"A{r}"], w=["xbf"], tag="h1cast")
            transpose_into_xfm(r, xbf)

        if dbg_stop == "p5" and blk == nblk - 1:
            break
        sch.claim("vector", nop, AR1_ATTN + ["h"], tag="claimH")

        for g2 in range(NI // 512):
            for side in range(2):
                bks = (0, 1, 2, 3) if side == 0 else (4, 5, 6, 7)
                for kq in range(4):
                    for j in range(2):
                        qslot, stag = stream_half(sguP_d, 22, kq,
                                                  11 * side + g2, j)
                        for kcl in range(2 * j, 2 * j + 2):
                            kc = 4 * kq + kcl
                            slot = qslot[:, 512 * (kcl - 2 * j):512 * (kcl - 2 * j) + 512]
                            for m in range(4):
                                OP("tensor", lambda e, slot=slot, kc=kc, m=m,
                                   bks=bks: e.matmul(
                                       pb[bks[m]][:, :],
                                       slot[:, 128 * m:128 * (m + 1)],
                                       xfm[:, TB * kc:TB * (kc + 1)],
                                       start=(kc == 0), stop=(kc == NEC - 1)),
                                   r=[stag, "xfm"], w=[f"ps{bks[m]}"],
                                   lazy=(m < 3), tag="gu_mm")
            for m in range(4):
                ic = 4 * g2 + m
                OP("scalar", lambda e, m=m: e.activation(
                    gs[:], pb[m][:, :], AF.Silu, bias=cc(C_ZERO),
                    scale=cc(C_ONE)),
                    r=[f"ps{m}", "cst"], w=["gs"], tag="silu")
                OP("vector", lambda e, ic=ic, m=m: e.tensor_tensor(
                    h_fm[:, TB * ic:TB * (ic + 1)], gs[:], pb[4 + m][:, :],
                    ALU.mult), r=["gs", f"ps{4 + m}"], w=["h"], tag="silmul")

        if dbg_stop == "p6" and blk == nblk - 1:
            break
        for og in range(4):
            bks = (0, 1, 2, 3) if og % 2 == 0 else (4, 5, 6, 7)
            for icq in range(NIC // 4):
                for j in range(2):
                    qslot, stag = stream_half(sdnP_d, 4, icq, og, j)
                    for icl in range(2 * j, 2 * j + 2):
                        ic = 4 * icq + icl
                        slot = qslot[:, 512 * (icl - 2 * j):512 * (icl - 2 * j) + 512]
                        for r in range(4):
                            bk = bks[r]
                            OP("tensor", lambda e, slot=slot, ic=ic, r=r, bk=bk:
                               e.matmul(pb[bk][:, :],
                                        h_fm[:, TB * ic + 128 * r:TB * ic + 128 * (r + 1)],
                                        slot[:, :], start=(ic == 0),
                                        stop=(ic == NIC - 1)),
                               r=[stag, "h"], w=[f"ps{bk}"], lazy=(r < 3),
                               tag="dn_mm")
            for r in range(4):
                bk = bks[r]
                OP("scalar", lambda e, r=r, og=og, bk=bk: e.copy(
                    B[:, E * r + 512 * og:E * r + 512 * (og + 1)], pb[bk][:, :]),
                    r=[f"ps{bk}"], w=[f"B{r}"], tag="y_ev")
        if dbg_stop == "p7a" and blk == nblk - 1:
            break
        for r in range(NTT):
            Ar = A[:, E * r:E * (r + 1)]
            Br = B[:, E * r:E * (r + 1)]
            Wr = wbF[:, 0:E]
            dum = ar1[:, 2048 * r:2048 * (r + 1)]
            sc = 24 + 10 * r

            def sl(i, sc=sc):
                return stats[:, sc + i:sc + i + 1]

            def sn(i, r=r):
                return f"su{i}_{r}"
            OP("scalar", lambda e, Br=Br, dum=dum, sc=sc: e.activation(
                dum, Br[:, :], AF.Square, bias=cc(C_ZERO), scale=cc(C_ONE),
                accum_out=sl(0, sc)),
                r=[f"B{r}", "cst"], w=[sn(0), "h"], tag="rsq")
            OP("vector", lambda e, sc=sc: e.tensor_scalar(
                sl(1, sc), sl(0, sc), 1.0 / E, RMS_EPS, ALU.mult,
                ALU.add), r=[sn(0)], w=[sn(1)], small=True, tag="rms")
            OP("scalar", lambda e, sc=sc: e.activation(
                sl(2, sc), sl(1, sc), AF.Sqrt, bias=cc(C_ZERO),
                scale=cc(C_ONE)), r=[sn(1), "cst"], w=[sn(2)], small=True, tag="rsqt")
            OP("vector", lambda e, sc=sc: e.reciprocal(sl(3, sc), sl(2, sc)),
               r=[sn(2)], w=[sn(3)], small=True, tag="rinv")
            OP("vector", lambda e, Br=Br, Ar=Ar, Wr=Wr, sc=sc:
               e.scalar_tensor_tensor(
                   Wr[:, :], Br[:, :], sl(3, sc), Ar[:, :], ALU.mult, ALU.add),
               r=[f"B{r}", sn(3), f"A{r}"], w=["h"], tag="resid2")
            OP("vector", lambda e, Wr=Wr, sc=sc: e.reduce_sum(
                sl(4, sc), Wr[:, :], axis=AXX),
               r=["h"], w=[sn(4)], small=True, tag="l2sum")
            OP("vector", lambda e, sc=sc: e.tensor_scalar(
                sl(5, sc), sl(4, sc), 1.0 / E, None, ALU.mult),
               r=[sn(4)], w=[sn(5)], small=True, tag="l2mean")
            OP("vector", lambda e, Ar=Ar, Wr=Wr, sc=sc: e.tensor_scalar(
                Ar[:, :], Wr[:, :], sl(5, sc), None, ALU.subtract),
                r=["h", sn(5)], w=[f"A{r}"], tag="l2ctr")
            OP("scalar", lambda e, Ar=Ar, dum=dum, sc=sc: e.activation(
                dum, Ar[:, :], AF.Square, bias=cc(C_ZERO), scale=cc(C_ONE),
                accum_out=sl(6, sc)),
                r=[f"A{r}", "cst"], w=[sn(6), "h"], tag="l2sq")
            OP("vector", lambda e, sc=sc: e.tensor_scalar(
                sl(7, sc), sl(6, sc), 1.0 / E, LN_EPS, ALU.mult,
                ALU.add), r=[sn(6)], w=[sn(7)], small=True, tag="l2var")
            OP("scalar", lambda e, sc=sc: e.activation(
                sl(8, sc), sl(7, sc), AF.Sqrt, bias=cc(C_ZERO),
                scale=cc(C_ONE)), r=[sn(7), "cst"], w=[sn(8)], small=True, tag="l2std")
            OP("vector", lambda e, sc=sc: e.reciprocal(sl(9, sc), sl(8, sc)),
               r=[sn(8)], w=[sn(9)], small=True, tag="l2rstd")
            OP("vector", lambda e, Ar=Ar, Br=Br, sc=sc: e.tensor_scalar(
                Br[:, :], Ar[:, :], sl(9, sc), None, ALU.mult),
                r=[f"A{r}", sn(9)], w=[f"B{r}"], tag="l2app")
            OP("scalar", lambda e, r=r, Br=Br, tb0=tb0: e.dma_start(
                out=y_d[tb0 + 128 * r:tb0 + 128 * r + 128, :], in_=Br[:, :]),
                r=[f"B{r}"], dma=f"ys{r}", tag="ystore")

    sch.barrier()
    tapmap = dict(A=A, B=B, xfm=xfm, qf=qf, es=es, af=af, kf0=kf0, kf1=kf1,
                  vt0=vt0, vt1=vt1, h=ar1, wb=wb, xbf=xbf, stats=stats,
                  sdn=sdnP_d, sgu=sguP_d, sq=sqP_d)
    for (name, ap_out) in tap_outs:
        key = name.split("__")[1]
        sap = tapmap[key]
        n_r = ap_out.shape[0]
        OP("gpsimd", lambda e, ap_out=ap_out, sap=sap, n_r=n_r: e.dma_start(
            out=ap_out, in_=sap[0:n_r, :]), r=[], dma="tap", tag="tap")

    sch.barrier()
    print(f"[build] ops={len(sch.ops)} instrs~={sch.n_instr} "
          f"sbuf_free={nc.sbuf_bytes_remaining} psum_free={nc.psum_banks_remaining}")
    blk_cm = nc.Block()
    block = blk_cm.__enter__()
    sch.emit(block)
    blk_cm.__exit__(None, None, None)
    st.close()
    return nc


def _consts_np():
    ident = np.eye(128, dtype=ml_dtypes.bfloat16)
    onesc = np.ones((128, 1), dtype=ml_dtypes.bfloat16)
    onesr = np.ones((1, 128), dtype=ml_dtypes.bfloat16)
    r = np.arange(128)[:, None]
    u = np.arange(1024)[None, :]
    MP = (r >= (u - 512)).astype(ml_dtypes.bfloat16)
    MO = (r <= (u - 512)).astype(ml_dtypes.bfloat16)
    mask = np.concatenate([MP, MO], axis=1)
    cst = np.tile(np.array(CONSTS, dtype=np.float32), (128, 1))
    return ident, onesc, onesr, mask, cst


def _pi(n):
    out = np.zeros(n, dtype=np.int64)
    b0 = 0
    while b0 < n:
        w = min(1024, n - b0)
        e = np.arange(w)
        j, kn = e // 8, e % 8
        t, hbit = kn // 2, kn % 2
        ep = (w // 2) * hbit + 4 * j + t
        out[b0 + ep] = b0 + e
        b0 += w
    return out


def make_in_map(xb, wqkv, wout, gup, gus, dnp, dns):
    ident, onesc, onesr, mask, cst = _consts_np()
    pi_e = _pi(E)
    pi_i = _pi(NI)
    rows = np.concatenate([pi_i, NI + pi_i])
    return dict(x=np.ascontiguousarray(xb),
                wqkv=np.ascontiguousarray(np.asarray(wqkv)[:, pi_e]),
                wout=np.asarray(wout),
                gup=np.ascontiguousarray(np.asarray(gup)[rows]),
                gus=np.ascontiguousarray(np.asarray(gus)[:, rows]),
                dnp=np.asarray(dnp),
                dns=np.pad(np.asarray(dns), ((0, 4), (0, 0))),
                identin=ident, onescin=onesc,
                onesrin=onesr, maskin=mask, cstin=cst)


def kernel(L0_attn_x, L0_attn_w_qkv, L0_attn_w_out, L0_ffn_gate_up_packed,
           L0_ffn_gate_up_scales, L0_ffn_down_packed, L0_ffn_down_scales):
    nc = build_program(nblk=8)
    in_maps = [make_in_map(L0_attn_x[b], L0_attn_w_qkv, L0_attn_w_out,
                           L0_ffn_gate_up_packed, L0_ffn_gate_up_scales,
                           L0_ffn_down_packed, L0_ffn_down_scales)
               for b in range(8)]
    res = run_bass_kernel_spmd(nc, in_maps, core_ids=list(range(8)))
    return np.stack([np.asarray(r["y"], dtype=np.float32)
                     for r in res.results])


# version 15
# speedup vs baseline: 1.0080x; 1.0080x over previous
import numpy as np
import ml_dtypes
import contextlib
import concourse.bass as bass
import concourse.mybir as mybir
from concourse.bass_utils import run_bass_kernel_spmd

F32 = mybir.dt.float32
BF16 = mybir.dt.bfloat16
F16 = mybir.dt.float16
I32 = mybir.dt.int32
U8 = mybir.dt.uint8
AF = mybir.ActivationFunctionType
ALU = mybir.AluOpType
AXX = mybir.AxisListType.X

S = 4096
E = 2048
NH = 16
NI = 5632
TB = 512
NTT = 4
NEC = 16
NIC = 44
INV_SQRT_HD = 1.0 / float(np.sqrt(128.0))
LN_EPS = 1e-5
RMS_EPS = 1e-6

C_ONE, C_EXPSC, C_M8, C_M10, C_SGS, C_SGB, C_ZERO, C_MIN1 = range(8)
CONSTS = [1.0, INV_SQRT_HD, -8.0, -10.0, -1.0 / 16.0, 7.5 / 16.0, 0.0,
          -1.0, 0, 0, 0, 0, 0, 0, 0, 0]

ENGS = ("sync", "gpsimd", "vector", "scalar", "tensor")


class Buf:
    __slots__ = ("name", "w", "readers", "w_small")

    def __init__(self, name):
        self.name = name
        self.w = None
        self.w_small = False
        self.readers = []


class Sem:
    __slots__ = ("name", "h", "n", "lazy")

    def __init__(self, name, h):
        self.name = name
        self.h = h
        self.n = 0
        self.lazy = False


class Sched:
    def __init__(self, nc=None, st=None):
        self.ops = []
        self.sems = {}
        self.water = {e: {} for e in ENGS}
        self.bufs = {}
        self.n_instr = 0
        self.nc = nc
        self.st = st

    def gb(self, name):
        if name not in self.bufs:
            self.bufs[name] = Buf(name)
        return self.bufs[name]

    def sem_for(self, eng, dma):
        if dma:
            key = "d_" + dma
            if key not in self.sems:
                h = self.st.enter_context(self.nc.semaphore(key))
                self.sems[key] = Sem(key, h)
            return self.sems[key]
        key = {"vector": "eV", "scalar": "eA", "tensor": "eT",
               "gpsimd": "eP"}[eng]
        return self.sems[key]

    def op(self, eng, emit, r=(), w=(), dma=False, lazy=False, small=False,
           tag=""):
        sem = self.sem_for(eng, dma)
        amt = 16 if dma else 1
        reads = [self.gb(x) for x in r]
        writes = [self.gb(x) for x in w]
        deps = []
        for b in reads:
            if b.w is not None:
                deps.append((b.w[0], b.w[1], b.w_small))
        for b in writes:
            if b.w is not None:
                deps.append((b.w[0], b.w[1], False))
            deps.extend((s_, v_, False) for (s_, v_) in b.readers)
        deps = [(s, s.n if s.name.startswith("d_") else v, f)
                for (s, v, f) in deps]
        need = {}
        own = None if dma else sem
        for (s, v, force) in deps:
            if s is own and not force:
                continue
            if need.get(s, -1) < v:
                need[s] = v
        wm = self.water[eng]
        waits = []
        for s, v in sorted(need.items(), key=lambda kv: kv[0].name):
            if wm.get(s, -1) >= v:
                continue
            wm[s] = v
            waits.append((s, v))
        if lazy:
            assert not dma
            val = sem.n + 1
            sem.lazy = True
        else:
            sem.n += amt
            val = sem.n
            sem.lazy = False
        for b in writes:
            b.w = (sem, val)
            b.w_small = small
            b.readers = []
        for b in reads:
            b.readers.append((sem, val))
        self.ops.append((eng, waits, emit, None if lazy else sem, amt, tag))
        self.n_instr += 1 + len(waits)

    def claim(self, eng, emitnop, names, tag="claim"):
        self.op(eng, emitnop, r=(), w=names, tag=tag)

    def barrier(self):
        for s in self.sems.values():
            assert not s.lazy, f"lazy pending on {s.name} at barrier"
        cur = [(s, s.n) for s in self.sems.values() if s.n > 0]
        for e in ENGS:
            wm = self.water[e]
            todo = [(s, v) for (s, v) in cur if wm.get(s, -1) < v]
            for (s, v) in todo:
                wm[s] = v
            if todo:
                self.ops.append((e, todo, None, None, 0, "barrier"))
                self.n_instr += len(todo)

    def emit(self, block):
        def mk(name):
            def body(eng):
                for (e, waits, emit, sem, amt, tag) in self.ops:
                    if e != name:
                        continue
                    for (s, v) in waits:
                        eng.wait_ge(s.h, v)
                    if emit is None:
                        continue
                    ins = emit(eng)
                    if sem is not None:
                        ins.then_inc(sem.h, amt)
            return body
        block.sync(mk("sync"))
        block.gpsimd(mk("gpsimd"))
        block.vector(mk("vector"))
        block.scalar(mk("scalar"))
        block.tensor(mk("tensor"))


def build_program(nblk=8, taps=None, dbg_stop=None, rd=True):
    nc = bass.Bass("TRN2", target_bir_lowering=False,
                   detect_race_conditions=rd)
    st = contextlib.ExitStack()

    x_d = nc.dram_tensor("x", [S, E], F32, kind="ExternalInput").ap()
    wqkv_d = nc.dram_tensor("wqkv", [3 * E, E], F32, kind="ExternalInput").ap()
    wout_d = nc.dram_tensor("wout", [E, E], F32, kind="ExternalInput").ap()
    gup_d = nc.dram_tensor("gup", [2 * NI, E // 8], I32, kind="ExternalInput").ap()
    gus_d = nc.dram_tensor("gus", [16, 2 * NI], F16, kind="ExternalInput").ap()
    dnp_d = nc.dram_tensor("dnp", [E, NI // 8], I32, kind="ExternalInput").ap()
    dns_d = nc.dram_tensor("dns", [48, E], F16, kind="ExternalInput").ap()
    ident_d = nc.dram_tensor("identin", [128, 128], BF16, kind="ExternalInput").ap()
    onesc_d = nc.dram_tensor("onescin", [128, 1], BF16, kind="ExternalInput").ap()
    onesr_d = nc.dram_tensor("onesrin", [1, 128], BF16, kind="ExternalInput").ap()
    mask_d = nc.dram_tensor("maskin", [128, 2048], BF16, kind="ExternalInput").ap()
    cst_d = nc.dram_tensor("cstin", [128, 16], F32, kind="ExternalInput").ap()
    y_d = nc.dram_tensor("y", [S, E], F32, kind="ExternalOutput").ap()
    sqP_d = nc.dram_tensor("scr_qP", [4 * 12 * 128, 2048], BF16, kind="Internal").ap()
    soP_d = nc.dram_tensor("scr_oP", [4 * 4 * 128, 2048], BF16, kind="Internal").ap()
    sguP_d = nc.dram_tensor("scr_guP", [4 * 22 * 128, 2048], BF16, kind="Internal").ap()
    sdnP_d = nc.dram_tensor("scr_dnP", [11 * 4 * 128, 2048], BF16, kind="Internal").ap()

    tap_outs = []
    if taps:
        for (name, shape, dtype) in taps:
            tap_outs.append((name, nc.dram_tensor(name, shape, dtype,
                                                  kind="ExternalOutput").ap()))

    sb = lambda name, shape, dt: st.enter_context(nc.sbuf_tensor(name, shape, dt))
    A = sb("A", [128, 4 * E], F32)
    B = sb("B", [128, 4 * E], F32)
    xbf = sb("xbf", [128, E], BF16)
    xfm = sb("xfm", [128, NEC * TB], BF16)
    kf0 = sb("kf0", [128, NH * TB], BF16)
    kf1 = sb("kf1", [128, NH * TB], BF16)
    vt0 = sb("vt0", [128, NTT * E], BF16)
    vt1 = sb("vt1", [128, NTT * E], BF16)
    ar1 = sb("ar1", [128, NIC * TB], BF16)
    wb = sb("wb", [128, 8 * TB], BF16)
    mkp = sb("mkp", [128, 1024], BF16)
    mko = sb("mko", [128, 1024], BF16)
    gs = sb("gs", [128, TB], BF16)
    ident = sb("ident", [128, 128], BF16)
    onescol = sb("onescol", [128, 1], BF16)
    onesrow = sb("onesrow", [1, 128], BF16)
    cst = sb("cst", [128, 16], F32)
    stats = sb("stats", [128, 64], F32)

    qf = ar1[:, 0:NH * TB]
    es = ar1[:, NH * TB:NH * TB + 4 * TB]
    af = ar1[:, 20 * TB:36 * TB]
    recb = ar1[:, 36 * TB:38 * TB].bitcast(F32)
    rh = ar1[:, 38 * TB:39 * TB][0:1, 0:512]
    rl = ar1[:, 39 * TB:40 * TB][0:1, 0:512]
    srow = ar1[:, 40 * TB:42 * TB].bitcast(F32)[0:1, 0:512]
    h_fm = ar1
    wrF = [ar1[:, 4096 * r:4096 * (r + 1)].bitcast(F32) for r in range(4)]

    AR1_ATTN = ["qf", "esA", "es0", "es1", "es2", "es3", "af", "rec",
                "srow", "rhi", "rlo", "xb2"]
    xbf2 = ar1[:, 0:2048]

    pq = st.enter_context(nc.psum_tensor("pq", [128, 2048], F32))
    pb = [pq[:, 512 * i:512 * (i + 1)] for i in range(4)] + \
         [st.enter_context(nc.psum_tensor(f"pb{i}", [128, 512], F32))
          for i in range(4, 8)]
    pbh = {k: pb[k].bitcast(BF16) for k in (4, 5, 6, 7)}

    sch = Sched(nc, st)
    for sname in ("eV", "eA", "eT", "eP"):
        h = st.enter_context(nc.semaphore(sname))
        sch.sems[sname] = Sem(sname, h)

    OP = sch.op

    def cc(i):
        return cst[:, i:i + 1]

    nop = lambda e: e.engine_nop()

    OP("sync", lambda e: e.dma_start(out=ident[:], in_=ident_d), w=["ident"], dma="c")
    OP("sync", lambda e: e.dma_start(out=onescol[:], in_=onesc_d), w=["ones"], dma="c")
    OP("sync", lambda e: e.dma_start(out=onesrow[:], in_=onesr_d), w=["ones"], dma="c")
    OP("sync", lambda e: e.dma_start(out=mkp[:], in_=mask_d[:, 0:1024]), w=["mkp"], dma="c")
    OP("sync", lambda e: e.dma_start(out=mko[:], in_=mask_d[:, 1024:2048]), w=["mko"], dma="c")
    OP("sync", lambda e: e.dma_start(out=cst[:], in_=cst_d), w=["cst"], dma="c")

    stgA = xfm
    stgA5 = stgA.rearrange("p (g l o f) -> p g l o f", g=4, l=4, o=4)

    def tA_stage(dense, dense_tag, osub, tag):
        for g3 in range(4):
            k = 6 + g3 % 2
            for ecl in range(4):
                ec = 4 * g3 + ecl
                OP("tensor", lambda e, ec=ec, ecl=ecl, k=k, dense=dense:
                   e.transpose(pbh[k][:, 128 * ecl:128 * ecl + 128],
                               dense[:, 128 * ec:128 * ec + 128], ident[:]),
                   r=[dense_tag, "ident"], w=[f"ps{k}"], lazy=(ecl < 3),
                   tag=tag + "T")
            OP("scalar", lambda e, g3=g3, osub=osub, k=k:
               e.copy(stgA5[:, g3, :, osub, :],
                      pbh[k][:, 0:512].rearrange("p (l f) -> p l f", l=4)),
               r=[f"ps{k}"], w=[f"sa{g3}_{osub}"], tag=tag + "Tev")

    def tA_flush(dstP, nf, fg, tag):
        for g3 in range(4):
            rb = (g3 * nf + fg) * 128
            OP("scalar", lambda e, g3=g3, rb=rb, dstP=dstP: e.dma_start(
                out=dstP[rb:rb + 128, :],
                in_=stgA[:, 2048 * g3:2048 * g3 + 2048]),
                r=[f"sa{g3}_0", f"sa{g3}_1", f"sa{g3}_2", f"sa{g3}_3"],
                dma=f"fa{g3}", tag=tag + "st")

    w32 = (kf0.bitcast(F32)[:, 0:E], kf0.bitcast(F32)[:, E:2 * E],
           kf1.bitcast(F32)[:, 0:E], kf1.bitcast(F32)[:, E:2 * E])
    dstg = (vt0[:, 0:E], vt1[:, 0:E])

    emitA, loadA = [], []

    def mk_cast_unit(src_d, dstP, nf, c, j):
        sl4 = j % 4
        wsrc, wdst = w32[sl4], dstg[c % 2]
        ldb, stb = f"wld{sl4}", f"wst{c % 2}"

        def load():
            OP("gpsimd", lambda e, wsrc=wsrc, c=c, src_d=src_d: e.dma_start(
                out=wsrc[:, :], in_=src_d[128 * c:128 * c + 128, :]),
                w=[ldb], dma=f"wl{sl4}", tag="wc_ld")

        def unit():
            OP("vector", lambda e, wsrc=wsrc, wdst=wdst:
               e.tensor_copy(wdst[:, :], wsrc[:, :]),
               r=[ldb], w=[stb], tag="wc_cp")
            tA_stage(wdst, stb, c % 4, "wc")
            if c % 4 == 3:
                tA_flush(dstP, nf, c // 4, "wc")
        return load, unit

    _j = 0
    for c in range(3 * E // 128):
        ld, un = mk_cast_unit(wqkv_d, sqP_d, 12, c, _j)
        loadA.append(ld)
        emitA.append(un)
        _j += 1
    for c in range(E // 128):
        ld, un = mk_cast_unit(wout_d, soP_d, 4, c, _j)
        loadA.append(ld)
        emitA.append(un)
        _j += 1

    U16 = mybir.dt.uint16
    DQ = []
    for p in range(2):
        b0 = 4608 * p
        DQ.append(dict(
            by=ar1[:, b0:b0 + 512].bitcast(U16),
            t1=ar1[:, b0 + 512:b0 + 1024].bitcast(U16),
            t2=ar1[:, b0 + 1024:b0 + 1536].bitcast(U16),
            W=ar1[:, b0 + 1536:b0 + 2560],
            G=ar1[:, b0 + 2560:b0 + 3584],
            C=ar1[:, b0 + 3584:b0 + 4608],
        ))
    guld = ar1[:, 9216:10240].bitcast(I32)
    dnld = ar1[:, 10240:11648].bitcast(I32)
    stgG = ar1[:, 11648:19840]
    stgG5 = stgG.rearrange("p (g l o f) -> p g l o f", g=4, l=4, o=4)
    stgD = ar1[:, 19840:21888]
    sc_gu = [ar1[:, 21888 + 32 * p:21920 + 32 * p].bitcast(F32) for p in range(2)]
    sc_guh = [ar1[:, 21952 + 16 * p:21968 + 16 * p].bitcast(F16) for p in range(2)]
    sc_dn = [ar1[:, 21984 + 88 * p:22072 + 88 * p].bitcast(F32) for p in range(2)]
    sc_dnh = [ar1[:, 22160 + 48 * p:22208 + 48 * p].bitcast(F16) for p in range(2)]

    tcount = [0]

    def dq_tile(words_ap, nwords, scales_ap, sctag, ldtag, store_fn):
        p = tcount[0] % 2
        tcount[0] += 1
        d = DQ[p]
        nb = 4 * nwords
        ne = 8 * nwords
        by = d["by"][:, 0:nb]
        t1 = d["t1"][:, 0:nb]
        t2 = d["t2"][:, 0:nb]
        Wu = d["W"].bitcast(U16)
        Gu = d["G"].bitcast(U16)
        Wb = d["W"][:, 0:ne]
        Gb = d["G"][:, 0:ne]
        Cb = d["C"][:, 0:ne]
        nby, nt1, nt2 = f"q{p}by", f"q{p}t1", f"q{p}t2"
        nW, nG, nC = f"q{p}W", f"q{p}G", f"q{p}C"
        OP("vector", lambda e: e.tensor_copy(by[:, :], words_ap.bitcast(U8)),
           r=[ldtag], w=[nby], tag="dq_by")
        for h in range(2):
            Wh = Wu[:, h * nb:h * nb + nb]
            Gh = Gu[:, h * nb:h * nb + nb]
            if h == 0:
                OP("vector", lambda e: e.tensor_scalar(
                    t1[:, :], by[:, :], 15, 6, ALU.bitwise_and,
                    ALU.logical_shift_left), r=[nby], w=[nt1], tag="dq_t1")
                OP("vector", lambda e: e.tensor_scalar(
                    t2[:, :], by[:, :], 8, 6, ALU.bitwise_and,
                    ALU.logical_shift_left), r=[nby], w=[nt2], tag="dq_t2")
            else:
                OP("vector", lambda e: e.tensor_scalar(
                    t1[:, :], by[:, :], 4, 6, ALU.logical_shift_right,
                    ALU.logical_shift_left), r=[nby], w=[nt1], tag="dq_t1")
                OP("vector", lambda e: e.tensor_scalar(
                    t2[:, :], by[:, :], 128, 2, ALU.bitwise_and,
                    ALU.logical_shift_left), r=[nby], w=[nt2], tag="dq_t2")
            OP("vector", lambda e, Wh=Wh: e.tensor_tensor(
                Wh[:, :], t1[:, :], t2[:, :], ALU.subtract),
               r=[nt1, nt2], w=[nW], tag="dq_sub")
            OP("vector", lambda e, Wh=Wh: e.tensor_scalar(
                Wh[:, :], Wh[:, :], 16128, None, ALU.add),
               r=[nW], w=[nW], tag="dq_add")
            OP("vector", lambda e, Gh=Gh: e.tensor_scalar(
                Gh[:, :], t2[:, :], 6, 16256, ALU.logical_shift_left,
                ALU.bitwise_or), r=[nt2], w=[nG], tag="dq_g")
        OP("scalar", lambda e: e.activation(
            Cb[:, :], Wb[:, :], AF.Relu, bias=cc(C_ONE), scale=cc(C_MIN1)),
           r=[nW, "cst"], w=[nC], tag="dq_relu")
        OP("vector", lambda e: e.tensor_tensor(
            Wb[:, :], Wb[:, :], Cb[:, :], ALU.subtract),
           r=[nW, nC], w=[nW], tag="dq_mag")
        OP("vector", lambda e: e.tensor_tensor(
            Wb[:, :], Wb[:, :], Gb[:, :], ALU.mult),
           r=[nW, nG], w=[nW], tag="dq_val")
        ng = ne // 128
        W3 = Wb.rearrange("p (hh c) -> p hh c", hh=2)
        for g in range(ng):
            OP("vector", lambda e, g=g: e.tensor_scalar(
                W3[:, :, 64 * g:64 * (g + 1)], W3[:, :, 64 * g:64 * (g + 1)],
                scales_ap[:, g:g + 1], None, ALU.mult),
                r=[sctag], w=[nW], lazy=(g < ng - 1), tag="dq_sc")
        store_fn(Wb, nW, p)

    def mk_gu_unit(c):
        sl = c % 2
        wsl = guld[:, 256 * sl:256 * sl + 256]

        def load():
            OP("sync", lambda e, c=c, wsl=wsl: e.dma_start(
                out=wsl[:, :], in_=gup_d[128 * c:128 * c + 128, :]),
                w=[f"gld{sl}"], dma=f"gw{sl}", tag="gu_ld")
            OP("sync", lambda e, c=c: e.dma_start(
                out=sc_guh[sl][:, :], in_=gus_d[:, 128 * c:128 * c + 128],
                transpose=True), w=[f"gsch{sl}"], dma=f"gs{sl}", tag="gu_scld")

        def unit():
            OP("vector", lambda e: e.tensor_copy(sc_gu[sl][:, :],
                                                 sc_guh[sl][:, :]),
               r=[f"gsch{sl}"], w=[f"gscf{sl}"], small=True, tag="gu_scvt")
            for half in range(2):
                wv = wsl[:, 128 * half:128 * (half + 1)]

                def store_gu(dense, dtag, p, c=c, half=half):
                    osub = c % 4
                    for g3 in range(2 * half, 2 * half + 2):
                        k = 4 + g3 % 2
                        for ecl in range(4):
                            el = (g3 - 2 * half) * 4 + ecl
                            OP("tensor", lambda e, el=el, ecl=ecl, k=k,
                               dense=dense: e.transpose(
                                   pbh[k][:, 128 * ecl:128 * ecl + 128],
                                   dense[:, 128 * el:128 * el + 128],
                                   ident[:]),
                               r=[dtag, "ident"], w=[f"ps{k}"],
                               lazy=(ecl < 3), tag="guT")
                        OP("scalar", lambda e, g3=g3, osub=osub, k=k: e.copy(
                            stgG5[:, g3, :, osub, :],
                            pbh[k][:, 0:512].rearrange("p (l f) -> p l f",
                                                       l=4)),
                            r=[f"ps{k}"], w=[f"sg{g3}_{osub}"], tag="guTev")
                    if c % 4 == 3 and half == 1:
                        colidx = c // 4
                        for g3 in range(4):
                            rb = (g3 * 22 + colidx) * 128
                            OP("scalar", lambda e, g3=g3, rb=rb: e.dma_start(
                                out=sguP_d[rb:rb + 128, :],
                                in_=stgG[:, 2048 * g3:2048 * g3 + 2048]),
                                r=[f"sg{g3}_0", f"sg{g3}_1", f"sg{g3}_2",
                                   f"sg{g3}_3"], dma=f"fg{g3}", tag="gu_st")

                dq_tile(wv, 128, sc_gu[sl][:, 8 * half:8 * half + 8],
                        f"gscf{sl}", f"gld{sl}", store_gu)
        return load, unit

    loadG, emitG = zip(*[mk_gu_unit(c) for c in range(2 * NI // 128)])

    def mk_dn_unit(c):
        sl = c % 2

        def unit():
            OP("sync", lambda e, c=c: e.dma_start(
                out=dnld[:, :], in_=dnp_d[128 * c:128 * c + 128, :]),
                w=["dld"], dma="gw0", tag="dn_ld")
            for tt in range(3):
                OP("sync", lambda e, c=c, tt=tt: e.dma_start(
                    out=sc_dnh[sl][:, 16 * tt:16 * tt + 16],
                    in_=dns_d[16 * tt:16 * tt + 16, 128 * c:128 * c + 128],
                    transpose=True), w=[f"dsch{sl}"], dma=f"gs{sl}",
                    tag="dn_scld")
            OP("vector", lambda e: e.tensor_copy(sc_dn[sl][:, :],
                                                 sc_dnh[sl][:, 0:44]),
               r=[f"dsch{sl}"], w=[f"dscf{sl}"], small=True, tag="dn_scvt")
            for bq in range(6):
                nwords = 128 if bq < 5 else 64
                wv = dnld[:, 128 * bq:128 * bq + nwords]

                def store_dn(dense, dtag, p, c=c, bq=bq, nwords=nwords):
                    for g in range(nwords // 64):
                        k = 4 + (bq * 2 + g) % 2
                        for ecl in range(4):
                            el = 4 * g + ecl
                            OP("tensor", lambda e, el=el, ecl=ecl, k=k,
                               dense=dense: e.transpose(
                                   pbh[k][:, 128 * ecl:128 * ecl + 128],
                                   dense[:, 128 * el:128 * el + 128],
                                   ident[:]),
                               r=[dtag, "ident"], w=[f"ps{k}"],
                               lazy=(ecl < 3), tag="dnT")
                        OP("scalar", lambda e, p=p, g=g, k=k: e.copy(
                            stgD[:, 1024 * p + 512 * g:1024 * p + 512 * g + 512],
                            pbh[k][:, 0:512]),
                            r=[f"ps{k}"], w=[f"sd{p}_{g}"], tag="dnTev")
                        for i in range(4):
                            icg = 8 * bq + 4 * g + i
                            rb = ((icg // 4) * 4 + c // 4) * 128
                            co = 512 * (icg % 4) + 128 * (c % 4)
                            OP("scalar", lambda e, rb=rb, co=co, p=p, g=g, i=i:
                               e.dma_start(
                                   out=sdnP_d[rb:rb + 128, co:co + 128],
                                   in_=stgD[:, 1024 * p + 512 * g + 128 * i:
                                            1024 * p + 512 * g + 128 * i + 128]),
                               r=[f"sd{p}_{g}"], dma=f"fd{icg % 4}",
                               tag="dn_st")

                dq_tile(wv, nwords, sc_dn[sl][:, 8 * bq:8 * bq + nwords // 16],
                        f"dscf{sl}", "dld", store_dn)
        return unit

    emitD = [mk_dn_unit(c) for c in range(E // 128)]

    na, ngu = len(emitA), len(emitG)
    done_a = la = lg = 0
    for i, gfn in enumerate(emitG):
        while lg <= min(i + 1, ngu - 1):
            loadG[lg]()
            lg += 1
        gfn()
        want = ((i + 1) * na) // ngu
        while la < min(want + 3, na):
            loadA[la]()
            la += 1
        while done_a < want:
            emitA[done_a]()
            done_a += 1
    while done_a < na:
        while la < min(done_a + 4, na):
            loadA[la]()
            la += 1
        emitA[done_a]()
        done_a += 1
    for dfn in emitD:
        dfn()

    sch.barrier()

    wcount = [0]

    def stream_half(srcP, nf, kq, colidx, j):
        i = wcount[0] % 4
        wcount[0] += 1
        half = wb[:, 1024 * i:1024 * i + 1024]
        rb = (kq * nf + colidx) * 128
        eng = "sync" if i % 2 == 0 else "gpsimd"
        OP(eng, lambda e, half=half, rb=rb, j=j, srcP=srcP: e.dma_start(
            out=half, in_=srcP[rb:rb + 128, 1024 * j:1024 * j + 1024]),
            w=[f"wq{i}"], dma=f"wq{i}", tag="wstream")
        return half, f"wq{i}"

    WBALL = ["wq0", "wq1", "wq2", "wq3"]

    def transpose_into_xfm(r, src_bf, srctag="xbf"):
        x5 = src_bf.rearrange("p (b J t h) -> p b h J t", b=2, J=128, t=4, h=2)
        for ec in range(NEC):
            b_, hh, q = ec // 8, (ec % 8) // 4, ec % 4
            k = 6 + (ec % 2)
            OP("tensor", lambda e, b_=b_, hh=hh, q=q, k=k, x5=x5:
               e.transpose(pbh[k][:, 0:128],
                           x5[:, b_, hh, 32 * q:32 * (q + 1), :], ident[:]),
               r=[srctag, "ident"], w=[f"ps{k}"], tag="xT")
            OP("scalar", lambda e, ec=ec, r=r, k=k:
               e.copy(xfm[:, TB * ec + 128 * r:TB * ec + 128 * r + 128],
                      pbh[k][:, 0:128]),
               r=[f"ps{k}"], w=["xfm"], tag="xTev")

    kf = (kf0, kf1)
    vt = (vt0, vt1)

    for blk in range(nblk):
        par = blk % 2
        kfC, kfP = kf[par], kf[1 - par]
        vtC, vtP = vt[par], vt[1 - par]
        kfCt, kfPt = f"kf{par}", f"kf{1 - par}"
        vtCt, vtPt = f"vt{par}", f"vt{1 - par}"
        tb0 = blk * TB

        sch.claim("vector", nop, ["h", "wr0", "wr1", "wr2", "wr3"] + AR1_ATTN,
                  tag="claimA")

        for r in range(NTT):
            Br = B[:, E * r:E * (r + 1)]
            OP("scalar", lambda e, r=r, Br=Br, tb0=tb0: e.dma_start(
                out=Br[:, :], in_=x_d[tb0 + 128 * r:tb0 + 128 * r + 128, :]),
                w=[f"B{r}"], dma=f"xl{r}", tag="xload")
            OP("vector", lambda e, Br=Br: e.tensor_copy(xbf[:], Br[:, :]),
               r=[f"B{r}"], w=["xbf"], tag="xcast")
            transpose_into_xfm(r, xbf)

        if dbg_stop == "p1" and blk == nblk - 1:
            break
        for fg in range(12):
            bks = (0, 1, 2, 3) if fg % 2 == 0 else (4, 5, 6, 7)
            for kq in range(4):
                for j in range(2):
                    qslot, stag = stream_half(sqP_d, 12, kq, fg, j)
                    for kcl in range(2 * j, 2 * j + 2):
                        kc = 4 * kq + kcl
                        slot = qslot[:, 512 * (kcl - 2 * j):512 * (kcl - 2 * j) + 512]
                        for m in range(4):
                            first, last = kc == 0, kc == NEC - 1
                            bk = bks[m]
                            if fg < 8:
                                OP("tensor", lambda e, slot=slot, kc=kc, m=m, bk=bk,
                                   first=first, last=last: e.matmul(
                                       pb[bk][:, :], slot[:, 128 * m:128 * (m + 1)],
                                       xfm[:, TB * kc:TB * (kc + 1)],
                                       start=first, stop=last),
                                   r=[stag, "xfm"], w=[f"ps{bk}"], lazy=(m < 3),
                                   tag="qk_mm")
                            else:
                                OP("tensor", lambda e, slot=slot, kc=kc, m=m, bk=bk,
                                   first=first, last=last: e.matmul(
                                       pb[bk][:, :],
                                       xfm[:, TB * kc + 128 * m:TB * kc + 128 * (m + 1)],
                                       slot[:, :], start=first, stop=last),
                                   r=[stag, "xfm"], w=[f"ps{bk}"], lazy=(m < 3),
                                   tag="v_mm")
            for m in range(4):
                bk = bks[m]
                if fg < 4:
                    hch = 4 * fg + m
                    OP("scalar", lambda e, hch=hch, bk=bk:
                       e.copy(qf[:, TB * hch:TB * (hch + 1)], pb[bk][:, :]),
                       r=[f"ps{bk}"], w=["qf"], tag="q_ev")
                elif fg < 8:
                    hch = 4 * (fg - 4) + m
                    OP("scalar", lambda e, hch=hch, bk=bk, kfC=kfC:
                       e.copy(kfC[:, TB * hch:TB * (hch + 1)], pb[bk][:, :]),
                       r=[f"ps{bk}"], w=[kfCt], tag="k_ev")
                else:
                    vg = fg - 8
                    OP("scalar", lambda e, vg=vg, m=m, bk=bk, vtC=vtC:
                       e.copy(vtC[:, E * m + 512 * vg:E * m + 512 * (vg + 1)],
                              pb[bk][:, :]),
                       r=[f"ps{bk}"], w=[vtCt], tag="v_ev")

        if dbg_stop == "p2" and blk == nblk - 1:
            break
        c_lo = 4 if blk == 0 else 0
        for hh in range(NH):
            pvb = 4 + (hh % 2)
            for half in range(c_lo // 4, 2):
                for cl in range(4):
                    c = 4 * half + cl
                    if c < 4:
                        ksrc, ktag = kfP, kfPt
                        kcol = TB * hh + 128 * c
                    else:
                        ksrc, ktag = kfC, kfCt
                        kcol = TB * hh + 128 * (c - 4)
                    ql = max(0, 128 * (c - 4))
                    qh = min(512, 128 * (c + 1))
                    OP("tensor", lambda e, ksrc=ksrc, kcol=kcol, cl=cl, hh=hh,
                       ql=ql, qh=qh:
                       e.matmul(pb[cl][:, ql:qh], ksrc[:, kcol:kcol + 128],
                                qf[:, TB * hh + ql:TB * hh + qh], start=True,
                                stop=True),
                       r=[ktag, "qf"], w=[f"ps{cl}"], lazy=(cl < 3),
                       tag="score")
                OP("scalar", lambda e: e.activation(
                    es[:, :], pq[:, :], AF.Exp, bias=cc(C_ZERO),
                    scale=cc(C_EXPSC)),
                    r=["ps0", "ps1", "ps2", "ps3", "cst"],
                    w=["esA", "es0", "es1", "es2", "es3"], tag="exp")
                for cl in range(4):
                    c = 4 * half + cl
                    if c < 4:
                        msrc, mt = mkp, "mkp"
                        mcol = 512 - 128 * c
                        vsrc, vtag = vtP, vtPt
                        vcol = E * c + 128 * hh
                    else:
                        msrc, mt = mko, "mko"
                        mcol = 512 - 128 * (c - 4)
                        vsrc, vtag = vtC, vtCt
                        vcol = E * (c - 4) + 128 * hh
                    OP("vector", lambda e, cl=cl, msrc=msrc, mcol=mcol:
                       e.tensor_tensor(es[:, 512 * cl:512 * cl + 512],
                                       es[:, 512 * cl:512 * cl + 512],
                                       msrc[:, mcol:mcol + 512], ALU.mult),
                       r=["esA", mt], w=[f"es{cl}"], tag="mask")
                    OP("tensor", lambda e, cl=cl, c=c, c_lo=c_lo: e.matmul(
                        pb[6][0:1, :], onescol[:, 0:1],
                        es[:, 512 * cl:512 * cl + 512],
                        start=(c == c_lo), stop=(c == 7)),
                        r=[f"es{cl}", "ones"], w=["ps6"], lazy=(c < 7),
                        tag="summ")
                    OP("tensor", lambda e, vsrc=vsrc, vcol=vcol, cl=cl, c=c,
                       c_lo=c_lo, pvb=pvb:
                       e.matmul(pb[pvb][:, :], vsrc[:, vcol:vcol + 128],
                                es[:, 512 * cl:512 * cl + 512],
                                start=(c == c_lo), stop=(c == 7)),
                       r=[vtag, f"es{cl}"], w=[f"ps{pvb}"], lazy=(c < 7),
                       tag="pv")
            OP("scalar", lambda e: e.copy(srow[:, :], pb[6][0:1, :]),
               r=["ps6"], w=["srow"], small=True, tag="sum_ev")
            OP("vector", lambda e: e.tensor_copy(rh[:, :], srow[:, :]),
               r=["srow"], w=["rhi"], small=True, tag="rhi")
            OP("vector", lambda e: e.scalar_tensor_tensor(
                rl[:, :], rh[:, :], -1.0, srow[:, :], ALU.mult, ALU.add),
                r=["rhi", "srow"], w=["rlo"], small=True, tag="rlo")
            OP("tensor", lambda e: e.matmul(pb[7][:, :], onesrow[:, :],
                                            rh[:, :], start=True, stop=False),
               r=["rhi", "ones"], w=["ps7"], lazy=True, tag="bch")
            OP("tensor", lambda e: e.matmul(pb[7][:, :], onesrow[:, :],
                                            rl[:, :], start=False, stop=True),
               r=["rlo", "ones"], w=["ps7"], tag="bcl")
            OP("vector", lambda e: e.reciprocal(recb[:, :], pb[7][:, :]),
               r=["ps7"], w=["rec"], tag="recip")
            OP("vector", lambda e, hh=hh, pvb=pvb: e.scalar_tensor_tensor(
                af[:, TB * hh:TB * (hh + 1)], pb[pvb][:, :], 1.0, recb[:, :],
                ALU.bypass, ALU.mult),
                r=[f"ps{pvb}", "rec"], w=["af"], tag="attn_ev")

        if dbg_stop == "p3" and blk == nblk - 1:
            break
        for og in range(4):
            bks = (0, 1, 2, 3) if og % 2 == 0 else (4, 5, 6, 7)
            for kq in range(4):
                for j in range(2):
                    qslot, stag = stream_half(soP_d, 4, kq, og, j)
                    for kcl in range(2 * j, 2 * j + 2):
                        kc = 4 * kq + kcl
                        slot = qslot[:, 512 * (kcl - 2 * j):512 * (kcl - 2 * j) + 512]
                        for m in range(4):
                            bk = bks[m]
                            OP("tensor", lambda e, slot=slot, kc=kc, m=m, bk=bk:
                               e.matmul(pb[bk][:, :],
                                        af[:, TB * kc + 128 * m:TB * kc + 128 * (m + 1)],
                                        slot[:, :], start=(kc == 0),
                                        stop=(kc == NEC - 1)),
                               r=[stag, "af"], w=[f"ps{bk}"], lazy=(m < 3),
                               tag="pj_mm")
            for r in range(4):
                bk = bks[r]
                OP("vector", lambda e, r=r, og=og, bk=bk: e.scalar_tensor_tensor(
                    A[:, E * r + 512 * og:E * r + 512 * (og + 1)], pb[bk][:, :],
                    1.0, B[:, E * r + 512 * og:E * r + 512 * (og + 1)],
                    ALU.bypass, ALU.add),
                    r=[f"ps{bk}", f"B{r}"], w=[f"A{r}"], tag="resid1")
        if dbg_stop == "p4a" and blk == nblk - 1:
            break
        def mk_ln1(r):
            Ar = A[:, E * r:E * (r + 1)]
            Br = B[:, E * r:E * (r + 1)]
            sc = 6 * r
            cb, cbt = (xbf, "xbf") if r % 2 == 0 else (xbf2, "xb2")

            def st(i):
                return stats[:, sc + i:sc + i + 1]

            def nm(i):
                return f"st{i}_{r}"
            return [
                lambda: OP("vector", lambda e: e.reduce_sum(st(0), Ar[:, :],
                                                            axis=AXX),
                           r=[f"A{r}"], w=[nm(0)], small=True, tag="l1sum"),
                lambda: OP("vector", lambda e: e.tensor_scalar(
                    st(1), st(0), 1.0 / E, None, ALU.mult),
                    r=[nm(0)], w=[nm(1)], small=True, tag="l1mean"),
                lambda: OP("vector", lambda e: e.tensor_scalar(
                    Br[:, :], Ar[:, :], st(1), None, ALU.subtract),
                    r=[f"A{r}", nm(1)], w=[f"B{r}"], tag="l1ctr"),
                lambda: OP("scalar", lambda e: e.activation(
                    xbf[:], Br[:, :], AF.Square, bias=cc(C_ZERO),
                    scale=cc(C_ONE), accum_out=st(2)),
                    r=[f"B{r}", "cst"], w=[nm(2), "xbf"], tag="l1sq"),
                lambda: OP("vector", lambda e: e.tensor_scalar(
                    st(3), st(2), 1.0 / E, LN_EPS, ALU.mult, ALU.add),
                    r=[nm(2)], w=[nm(3)], small=True, tag="l1var"),
                lambda: OP("scalar", lambda e: e.activation(
                    st(4), st(3), AF.Sqrt, bias=cc(C_ZERO), scale=cc(C_ONE)),
                    r=[nm(3), "cst"], w=[nm(4)], small=True, tag="l1std"),
                lambda: OP("vector", lambda e: e.reciprocal(st(5), st(4)),
                           r=[nm(4)], w=[nm(5)], small=True, tag="l1rstd"),
                lambda: OP("vector", lambda e: e.tensor_scalar(
                    Ar[:, :], Br[:, :], st(5), None, ALU.mult),
                    r=[f"B{r}", nm(5)], w=[f"A{r}"], tag="l1app"),
                lambda: (OP("vector", lambda e: e.tensor_copy(cb[:], Ar[:, :]),
                            r=[f"A{r}"], w=[cbt], tag="h1cast"),
                         transpose_into_xfm(r, cb, cbt)),
            ]

        ch1 = [mk_ln1(r) for r in range(NTT)]
        for si in range(len(ch1[0])):
            for r in range(NTT):
                ch1[r][si]()

        if dbg_stop == "p5" and blk == nblk - 1:
            break
        sch.claim("vector", nop, AR1_ATTN + ["h", "wr0", "wr1", "wr2", "wr3"],
                  tag="claimH")

        for g2 in range(NI // 512):
            for side in range(2):
                bks = (0, 1, 2, 3) if side == 0 else (4, 5, 6, 7)
                for kq in range(4):
                    for j in range(2):
                        qslot, stag = stream_half(sguP_d, 22, kq,
                                                  11 * side + g2, j)
                        for kcl in range(2 * j, 2 * j + 2):
                            kc = 4 * kq + kcl
                            slot = qslot[:, 512 * (kcl - 2 * j):512 * (kcl - 2 * j) + 512]
                            for m in range(4):
                                OP("tensor", lambda e, slot=slot, kc=kc, m=m,
                                   bks=bks: e.matmul(
                                       pb[bks[m]][:, :],
                                       slot[:, 128 * m:128 * (m + 1)],
                                       xfm[:, TB * kc:TB * (kc + 1)],
                                       start=(kc == 0), stop=(kc == NEC - 1)),
                                   r=[stag, "xfm"], w=[f"ps{bks[m]}"],
                                   lazy=(m < 3), tag="gu_mm")
            for m in range(4):
                ic = 4 * g2 + m
                OP("scalar", lambda e, m=m: e.activation(
                    gs[:], pb[m][:, :], AF.Silu, bias=cc(C_ZERO),
                    scale=cc(C_ONE)),
                    r=[f"ps{m}", "cst"], w=["gs"], tag="silu")
                OP("vector", lambda e, ic=ic, m=m: e.tensor_tensor(
                    h_fm[:, TB * ic:TB * (ic + 1)], gs[:], pb[4 + m][:, :],
                    ALU.mult), r=["gs", f"ps{4 + m}"], w=["h"], tag="silmul")

        if dbg_stop == "p6" and blk == nblk - 1:
            break
        for og in range(4):
            bks = (0, 1, 2, 3) if og % 2 == 0 else (4, 5, 6, 7)
            for icq in range(NIC // 4):
                for j in range(2):
                    qslot, stag = stream_half(sdnP_d, 4, icq, og, j)
                    for icl in range(2 * j, 2 * j + 2):
                        ic = 4 * icq + icl
                        slot = qslot[:, 512 * (icl - 2 * j):512 * (icl - 2 * j) + 512]
                        for r in range(4):
                            bk = bks[r]
                            OP("tensor", lambda e, slot=slot, ic=ic, r=r, bk=bk:
                               e.matmul(pb[bk][:, :],
                                        h_fm[:, TB * ic + 128 * r:TB * ic + 128 * (r + 1)],
                                        slot[:, :], start=(ic == 0),
                                        stop=(ic == NIC - 1)),
                               r=[stag, "h"], w=[f"ps{bk}"], lazy=(r < 3),
                               tag="dn_mm")
            for r in range(4):
                bk = bks[r]
                OP("scalar", lambda e, r=r, og=og, bk=bk: e.copy(
                    B[:, E * r + 512 * og:E * r + 512 * (og + 1)], pb[bk][:, :]),
                    r=[f"ps{bk}"], w=[f"B{r}"], tag="y_ev")
        if dbg_stop == "p7a" and blk == nblk - 1:
            break
        def mk_ln2(r, tb0=tb0):
            Ar = A[:, E * r:E * (r + 1)]
            Br = B[:, E * r:E * (r + 1)]
            Wr = wrF[r]
            wn = f"wr{r}"
            dum = ar1[:, 4096 * r:4096 * r + 2048]
            sc = 24 + 10 * r

            def sl(i):
                return stats[:, sc + i:sc + i + 1]

            def sn(i):
                return f"su{i}_{r}"
            return [
                lambda: OP("scalar", lambda e: e.activation(
                    dum, Br[:, :], AF.Square, bias=cc(C_ZERO),
                    scale=cc(C_ONE), accum_out=sl(0)),
                    r=[f"B{r}", "cst"], w=[sn(0), wn], tag="rsq"),
                lambda: OP("vector", lambda e: e.tensor_scalar(
                    sl(1), sl(0), 1.0 / E, RMS_EPS, ALU.mult, ALU.add),
                    r=[sn(0)], w=[sn(1)], small=True, tag="rms"),
                lambda: OP("scalar", lambda e: e.activation(
                    sl(2), sl(1), AF.Sqrt, bias=cc(C_ZERO), scale=cc(C_ONE)),
                    r=[sn(1), "cst"], w=[sn(2)], small=True, tag="rsqt"),
                lambda: OP("vector", lambda e: e.reciprocal(sl(3), sl(2)),
                           r=[sn(2)], w=[sn(3)], small=True, tag="rinv"),
                lambda: OP("vector", lambda e: e.scalar_tensor_tensor(
                    Wr[:, :], Br[:, :], sl(3), Ar[:, :], ALU.mult, ALU.add),
                    r=[f"B{r}", sn(3), f"A{r}"], w=[wn], tag="resid2"),
                lambda: OP("vector", lambda e: e.reduce_sum(
                    sl(4), Wr[:, :], axis=AXX),
                    r=[wn], w=[sn(4)], small=True, tag="l2sum"),
                lambda: OP("vector", lambda e: e.tensor_scalar(
                    sl(5), sl(4), 1.0 / E, None, ALU.mult),
                    r=[sn(4)], w=[sn(5)], small=True, tag="l2mean"),
                lambda: OP("vector", lambda e: e.tensor_scalar(
                    Ar[:, :], Wr[:, :], sl(5), None, ALU.subtract),
                    r=[wn, sn(5)], w=[f"A{r}"], tag="l2ctr"),
                lambda: OP("scalar", lambda e: e.activation(
                    dum, Ar[:, :], AF.Square, bias=cc(C_ZERO),
                    scale=cc(C_ONE), accum_out=sl(6)),
                    r=[f"A{r}", "cst"], w=[sn(6), wn], tag="l2sq"),
                lambda: OP("vector", lambda e: e.tensor_scalar(
                    sl(7), sl(6), 1.0 / E, LN_EPS, ALU.mult, ALU.add),
                    r=[sn(6)], w=[sn(7)], small=True, tag="l2var"),
                lambda: OP("scalar", lambda e: e.activation(
                    sl(8), sl(7), AF.Sqrt, bias=cc(C_ZERO), scale=cc(C_ONE)),
                    r=[sn(7), "cst"], w=[sn(8)], small=True, tag="l2std"),
                lambda: OP("vector", lambda e: e.reciprocal(sl(9), sl(8)),
                           r=[sn(8)], w=[sn(9)], small=True, tag="l2rstd"),
                lambda: OP("vector", lambda e: e.tensor_scalar(
                    Br[:, :], Ar[:, :], sl(9), None, ALU.mult),
                    r=[f"A{r}", sn(9)], w=[f"B{r}"], tag="l2app"),
                lambda: OP("scalar", lambda e: e.dma_start(
                    out=y_d[tb0 + 128 * r:tb0 + 128 * r + 128, :],
                    in_=Br[:, :]),
                    r=[f"B{r}"], dma=f"ys{r}", tag="ystore"),
            ]

        ch2 = [mk_ln2(r) for r in range(NTT)]
        for si in range(len(ch2[0])):
            for r in range(NTT):
                ch2[r][si]()

    sch.barrier()
    tapmap = dict(A=A, B=B, xfm=xfm, qf=qf, es=es, af=af, kf0=kf0, kf1=kf1,
                  vt0=vt0, vt1=vt1, h=ar1, wb=wb, xbf=xbf, stats=stats,
                  sdn=sdnP_d, sgu=sguP_d, sq=sqP_d)
    for (name, ap_out) in tap_outs:
        key = name.split("__")[1]
        sap = tapmap[key]
        n_r = ap_out.shape[0]
        OP("gpsimd", lambda e, ap_out=ap_out, sap=sap, n_r=n_r: e.dma_start(
            out=ap_out, in_=sap[0:n_r, :]), r=[], dma="tap", tag="tap")

    sch.barrier()
    print(f"[build] ops={len(sch.ops)} instrs~={sch.n_instr} "
          f"sbuf_free={nc.sbuf_bytes_remaining} psum_free={nc.psum_banks_remaining}")
    blk_cm = nc.Block()
    block = blk_cm.__enter__()
    sch.emit(block)
    blk_cm.__exit__(None, None, None)
    st.close()
    return nc


def _consts_np():
    ident = np.eye(128, dtype=ml_dtypes.bfloat16)
    onesc = np.ones((128, 1), dtype=ml_dtypes.bfloat16)
    onesr = np.ones((1, 128), dtype=ml_dtypes.bfloat16)
    r = np.arange(128)[:, None]
    u = np.arange(1024)[None, :]
    MP = (r >= (u - 512)).astype(ml_dtypes.bfloat16)
    MO = (r <= (u - 512)).astype(ml_dtypes.bfloat16)
    mask = np.concatenate([MP, MO], axis=1)
    cst = np.tile(np.array(CONSTS, dtype=np.float32), (128, 1))
    return ident, onesc, onesr, mask, cst


def _pi(n):
    out = np.zeros(n, dtype=np.int64)
    b0 = 0
    while b0 < n:
        w = min(1024, n - b0)
        e = np.arange(w)
        j, kn = e // 8, e % 8
        t, hbit = kn // 2, kn % 2
        ep = (w // 2) * hbit + 4 * j + t
        out[b0 + ep] = b0 + e
        b0 += w
    return out


def make_in_map(xb, wqkv, wout, gup, gus, dnp, dns):
    ident, onesc, onesr, mask, cst = _consts_np()
    pi_e = _pi(E)
    pi_i = _pi(NI)
    rows = np.concatenate([pi_i, NI + pi_i])
    return dict(x=np.ascontiguousarray(xb),
                wqkv=np.ascontiguousarray(np.asarray(wqkv)[:, pi_e]),
                wout=np.asarray(wout),
                gup=np.ascontiguousarray(np.asarray(gup)[rows]),
                gus=np.ascontiguousarray(np.asarray(gus)[:, rows]),
                dnp=np.asarray(dnp),
                dns=np.pad(np.asarray(dns), ((0, 4), (0, 0))),
                identin=ident, onescin=onesc,
                onesrin=onesr, maskin=mask, cstin=cst)


def kernel(L0_attn_x, L0_attn_w_qkv, L0_attn_w_out, L0_ffn_gate_up_packed,
           L0_ffn_gate_up_scales, L0_ffn_down_packed, L0_ffn_down_scales):
    nc = build_program(nblk=8)
    in_maps = [make_in_map(L0_attn_x[b], L0_attn_w_qkv, L0_attn_w_out,
                           L0_ffn_gate_up_packed, L0_ffn_gate_up_scales,
                           L0_ffn_down_packed, L0_ffn_down_scales)
               for b in range(8)]
    res = run_bass_kernel_spmd(nc, in_maps, core_ids=list(range(8)))
    return np.stack([np.asarray(r["y"], dtype=np.float32)
                     for r in res.results])


# version 16
# speedup vs baseline: 1.0236x; 1.0154x over previous
import numpy as np
import ml_dtypes
import contextlib
import concourse.bass as bass
import concourse.mybir as mybir
from concourse.bass_utils import run_bass_kernel_spmd

F32 = mybir.dt.float32
BF16 = mybir.dt.bfloat16
F16 = mybir.dt.float16
I32 = mybir.dt.int32
U8 = mybir.dt.uint8
AF = mybir.ActivationFunctionType
ALU = mybir.AluOpType
AXX = mybir.AxisListType.X

S = 4096
E = 2048
NH = 16
NI = 5632
TB = 512
NTT = 4
NEC = 16
NIC = 44
INV_SQRT_HD = 1.0 / float(np.sqrt(128.0))
LN_EPS = 1e-5
RMS_EPS = 1e-6

C_ONE, C_EXPSC, C_M8, C_M10, C_SGS, C_SGB, C_ZERO, C_MIN1 = range(8)
CONSTS = [1.0, INV_SQRT_HD, -8.0, -10.0, -1.0 / 16.0, 7.5 / 16.0, 0.0,
          -1.0, 0, 0, 0, 0, 0, 0, 0, 0]

ENGS = ("sync", "gpsimd", "vector", "scalar", "tensor")


class Buf:
    __slots__ = ("name", "w", "readers", "w_small")

    def __init__(self, name):
        self.name = name
        self.w = None
        self.w_small = False
        self.readers = []


class Sem:
    __slots__ = ("name", "h", "n", "lazy")

    def __init__(self, name, h):
        self.name = name
        self.h = h
        self.n = 0
        self.lazy = False


class Sched:
    def __init__(self, nc=None, st=None):
        self.ops = []
        self.sems = {}
        self.water = {e: {} for e in ENGS}
        self.bufs = {}
        self.n_instr = 0
        self.nc = nc
        self.st = st

    def gb(self, name):
        if name not in self.bufs:
            self.bufs[name] = Buf(name)
        return self.bufs[name]

    def sem_for(self, eng, dma):
        if dma:
            key = "d_" + dma
            if key not in self.sems:
                h = self.st.enter_context(self.nc.semaphore(key))
                self.sems[key] = Sem(key, h)
            return self.sems[key]
        key = {"vector": "eV", "scalar": "eA", "tensor": "eT",
               "gpsimd": "eP"}[eng]
        return self.sems[key]

    def op(self, eng, emit, r=(), w=(), dma=False, lazy=False, small=False,
           tag=""):
        sem = self.sem_for(eng, dma)
        amt = 16 if dma else 1
        reads = [self.gb(x) for x in r]
        writes = [self.gb(x) for x in w]
        deps = []
        for b in reads:
            if b.w is not None:
                deps.append((b.w[0], b.w[1], b.w_small))
        for b in writes:
            if b.w is not None:
                deps.append((b.w[0], b.w[1], False))
            deps.extend((s_, v_, False) for (s_, v_) in b.readers)
        deps = [(s, s.n if s.name.startswith("d_") else v, f)
                for (s, v, f) in deps]
        need = {}
        own = None if dma else sem
        for (s, v, force) in deps:
            if s is own and not force:
                continue
            if need.get(s, -1) < v:
                need[s] = v
        wm = self.water[eng]
        waits = []
        for s, v in sorted(need.items(), key=lambda kv: kv[0].name):
            if wm.get(s, -1) >= v:
                continue
            wm[s] = v
            waits.append((s, v))
        if lazy:
            assert not dma
            val = sem.n + 1
            sem.lazy = True
        else:
            sem.n += amt
            val = sem.n
            sem.lazy = False
        for b in writes:
            b.w = (sem, val)
            b.w_small = small
            b.readers = []
        for b in reads:
            b.readers.append((sem, val))
        self.ops.append((eng, waits, emit, None if lazy else sem, amt, tag))
        self.n_instr += 1 + len(waits)

    def claim(self, eng, emitnop, names, tag="claim"):
        self.op(eng, emitnop, r=(), w=names, tag=tag)

    def barrier(self):
        for s in self.sems.values():
            assert not s.lazy, f"lazy pending on {s.name} at barrier"
        cur = [(s, s.n) for s in self.sems.values() if s.n > 0]
        for e in ENGS:
            wm = self.water[e]
            todo = [(s, v) for (s, v) in cur if wm.get(s, -1) < v]
            for (s, v) in todo:
                wm[s] = v
            if todo:
                self.ops.append((e, todo, None, None, 0, "barrier"))
                self.n_instr += len(todo)

    def emit(self, block):
        def mk(name):
            def body(eng):
                for (e, waits, emit, sem, amt, tag) in self.ops:
                    if e != name:
                        continue
                    for (s, v) in waits:
                        eng.wait_ge(s.h, v)
                    if emit is None:
                        continue
                    ins = emit(eng)
                    if sem is not None:
                        ins.then_inc(sem.h, amt)
            return body
        block.sync(mk("sync"))
        block.gpsimd(mk("gpsimd"))
        block.vector(mk("vector"))
        block.scalar(mk("scalar"))
        block.tensor(mk("tensor"))


def build_program(nblk=8, taps=None, dbg_stop=None, rd=True):
    nc = bass.Bass("TRN2", target_bir_lowering=False,
                   detect_race_conditions=rd)
    st = contextlib.ExitStack()

    x_d = nc.dram_tensor("x", [S, E], F32, kind="ExternalInput").ap()
    wqkv_d = nc.dram_tensor("wqkv", [3 * E, E], F32, kind="ExternalInput").ap()
    wout_d = nc.dram_tensor("wout", [E, E], F32, kind="ExternalInput").ap()
    gup_d = nc.dram_tensor("gup", [2 * NI, E // 8], I32, kind="ExternalInput").ap()
    gus_d = nc.dram_tensor("gus", [16, 2 * NI], F16, kind="ExternalInput").ap()
    dnp_d = nc.dram_tensor("dnp", [E, NI // 8], I32, kind="ExternalInput").ap()
    dns_d = nc.dram_tensor("dns", [48, E], F16, kind="ExternalInput").ap()
    ident_d = nc.dram_tensor("identin", [128, 128], BF16, kind="ExternalInput").ap()
    onesc_d = nc.dram_tensor("onescin", [128, 1], BF16, kind="ExternalInput").ap()
    onesr_d = nc.dram_tensor("onesrin", [1, 128], BF16, kind="ExternalInput").ap()
    mask_d = nc.dram_tensor("maskin", [128, 2048], BF16, kind="ExternalInput").ap()
    cst_d = nc.dram_tensor("cstin", [128, 16], F32, kind="ExternalInput").ap()
    y_d = nc.dram_tensor("y", [S, E], F32, kind="ExternalOutput").ap()
    sqP_d = nc.dram_tensor("scr_qP", [4 * 12 * 128, 2048], BF16, kind="Internal").ap()
    soP_d = nc.dram_tensor("scr_oP", [4 * 4 * 128, 2048], BF16, kind="Internal").ap()
    sguP_d = nc.dram_tensor("scr_guP", [4 * 22 * 128, 2048], BF16, kind="Internal").ap()
    sdnP_d = nc.dram_tensor("scr_dnP", [11 * 4 * 128, 2048], BF16, kind="Internal").ap()

    tap_outs = []
    if taps:
        for (name, shape, dtype) in taps:
            tap_outs.append((name, nc.dram_tensor(name, shape, dtype,
                                                  kind="ExternalOutput").ap()))

    sb = lambda name, shape, dt: st.enter_context(nc.sbuf_tensor(name, shape, dt))
    A = sb("A", [128, 4 * E], F32)
    B = sb("B", [128, 4 * E], F32)
    xbf = sb("xbf", [128, E], BF16)
    xfm = sb("xfm", [128, NEC * TB], BF16)
    kf0 = sb("kf0", [128, NH * TB], BF16)
    kf1 = sb("kf1", [128, NH * TB], BF16)
    vt0 = sb("vt0", [128, NTT * E], BF16)
    vt1 = sb("vt1", [128, NTT * E], BF16)
    ar1 = sb("ar1", [128, NIC * TB], BF16)
    wb = sb("wb", [128, 8 * TB], BF16)
    mkp = sb("mkp", [128, 1024], BF16)
    mko = sb("mko", [128, 1024], BF16)
    gs = sb("gs", [128, TB], BF16)
    ident = sb("ident", [128, 128], BF16)
    onescol = sb("onescol", [128, 1], BF16)
    onesrow = sb("onesrow", [1, 128], BF16)
    cst = sb("cst", [128, 16], F32)
    stats = sb("stats", [128, 64], F32)

    qf = ar1[:, 0:NH * TB]
    es = ar1[:, NH * TB:NH * TB + 4 * TB]
    af = ar1[:, 20 * TB:36 * TB]
    recb = ar1[:, 36 * TB:38 * TB].bitcast(F32)
    rh = ar1[:, 38 * TB:39 * TB][0:1, 0:512]
    rl = ar1[:, 39 * TB:40 * TB][0:1, 0:512]
    srow = ar1[:, 40 * TB:42 * TB].bitcast(F32)[0:1, 0:512]
    h_fm = ar1
    wrF = [ar1[:, 4096 * r:4096 * (r + 1)].bitcast(F32) for r in range(4)]

    AR1_ATTN = ["qf", "esA", "es0", "es1", "es2", "es3", "af", "rec",
                "srow", "rhi", "rlo", "xb2"]
    xbf2 = ar1[:, 0:2048]

    pq = st.enter_context(nc.psum_tensor("pq", [128, 2048], F32))
    pb = [pq[:, 512 * i:512 * (i + 1)] for i in range(4)] + \
         [st.enter_context(nc.psum_tensor(f"pb{i}", [128, 512], F32))
          for i in range(4, 8)]
    pbh = {k: pb[k].bitcast(BF16) for k in (4, 5, 6, 7)}

    sch = Sched(nc, st)
    for sname in ("eV", "eA", "eT", "eP"):
        h = st.enter_context(nc.semaphore(sname))
        sch.sems[sname] = Sem(sname, h)

    OP = sch.op

    def cc(i):
        return cst[:, i:i + 1]

    nop = lambda e: e.engine_nop()

    OP("sync", lambda e: e.dma_start(out=ident[:], in_=ident_d), w=["ident"], dma="c")
    OP("sync", lambda e: e.dma_start(out=onescol[:], in_=onesc_d), w=["ones"], dma="c")
    OP("sync", lambda e: e.dma_start(out=onesrow[:], in_=onesr_d), w=["ones"], dma="c")
    OP("sync", lambda e: e.dma_start(out=mkp[:], in_=mask_d[:, 0:1024]), w=["mkp"], dma="c")
    OP("sync", lambda e: e.dma_start(out=mko[:], in_=mask_d[:, 1024:2048]), w=["mko"], dma="c")
    OP("sync", lambda e: e.dma_start(out=cst[:], in_=cst_d), w=["cst"], dma="c")

    stgA = xfm
    stgA5 = stgA.rearrange("p (g l o f) -> p g l o f", g=4, l=4, o=4)

    def tA_stage(dense, dense_tag, osub, tag):
        for g3 in range(4):
            k = 6 + g3 % 2
            for ecl in range(4):
                ec = 4 * g3 + ecl
                OP("tensor", lambda e, ec=ec, ecl=ecl, k=k, dense=dense:
                   e.transpose(pbh[k][:, 128 * ecl:128 * ecl + 128],
                               dense[:, 128 * ec:128 * ec + 128], ident[:]),
                   r=[dense_tag, "ident"], w=[f"ps{k}"], lazy=(ecl < 3),
                   tag=tag + "T")
            OP("scalar", lambda e, g3=g3, osub=osub, k=k:
               e.copy(stgA5[:, g3, :, osub, :],
                      pbh[k][:, 0:512].rearrange("p (l f) -> p l f", l=4)),
               r=[f"ps{k}"], w=[f"sa{g3}_{osub}"], tag=tag + "Tev")

    def tA_flush(dstP, nf, fg, tag):
        for g3 in range(4):
            rb = (g3 * nf + fg) * 128
            OP("scalar", lambda e, g3=g3, rb=rb, dstP=dstP: e.dma_start(
                out=dstP[rb:rb + 128, :],
                in_=stgA[:, 2048 * g3:2048 * g3 + 2048]),
                r=[f"sa{g3}_0", f"sa{g3}_1", f"sa{g3}_2", f"sa{g3}_3"],
                dma=f"fa{g3}", tag=tag + "st")

    w32 = (kf0.bitcast(F32)[:, 0:E], kf0.bitcast(F32)[:, E:2 * E],
           kf1.bitcast(F32)[:, 0:E], kf1.bitcast(F32)[:, E:2 * E])
    dstg = (vt0[:, 0:E], vt1[:, 0:E])

    emitA, loadA = [], []

    def mk_cast_unit(src_d, dstP, nf, c, j):
        sl4 = j % 4
        wsrc, wdst = w32[sl4], dstg[c % 2]
        ldb, stb = f"wld{sl4}", f"wst{c % 2}"

        def load():
            OP("gpsimd", lambda e, wsrc=wsrc, c=c, src_d=src_d: e.dma_start(
                out=wsrc[:, :], in_=src_d[128 * c:128 * c + 128, :]),
                w=[ldb], dma=f"wl{sl4}", tag="wc_ld")

        def unit():
            OP("vector", lambda e, wsrc=wsrc, wdst=wdst:
               e.tensor_copy(wdst[:, :], wsrc[:, :]),
               r=[ldb], w=[stb], tag="wc_cp")
            tA_stage(wdst, stb, c % 4, "wc")
            if c % 4 == 3:
                tA_flush(dstP, nf, c // 4, "wc")
        return load, unit

    _j = 0
    for c in range(3 * E // 128):
        ld, un = mk_cast_unit(wqkv_d, sqP_d, 12, c, _j)
        loadA.append(ld)
        emitA.append(un)
        _j += 1
    for c in range(E // 128):
        ld, un = mk_cast_unit(wout_d, soP_d, 4, c, _j)
        loadA.append(ld)
        emitA.append(un)
        _j += 1

    U16 = mybir.dt.uint16
    DQ = []
    for p in range(2):
        b0 = 4608 * p
        DQ.append(dict(
            by=ar1[:, b0:b0 + 512].bitcast(U16),
            t1=ar1[:, b0 + 512:b0 + 1024].bitcast(U16),
            t2=ar1[:, b0 + 1024:b0 + 1536].bitcast(U16),
            W=ar1[:, b0 + 1536:b0 + 2560],
            G=ar1[:, b0 + 2560:b0 + 3584],
            C=ar1[:, b0 + 3584:b0 + 4608],
        ))
    guld = ar1[:, 9216:10240].bitcast(I32)
    dnld = ar1[:, 10240:11648].bitcast(I32)
    stgG = ar1[:, 11648:19840]
    stgG5 = stgG.rearrange("p (g l o f) -> p g l o f", g=4, l=4, o=4)
    stgD = ar1[:, 19840:21888]
    sc_gu = [ar1[:, 21888 + 32 * p:21920 + 32 * p].bitcast(F32) for p in range(2)]
    sc_guh = [ar1[:, 21952 + 16 * p:21968 + 16 * p].bitcast(F16) for p in range(2)]
    sc_dn = [ar1[:, 21984 + 88 * p:22072 + 88 * p].bitcast(F32) for p in range(2)]
    sc_dnh = [ar1[:, 22160 + 48 * p:22208 + 48 * p].bitcast(F16) for p in range(2)]

    tcount = [0]

    def dq_tile(words_ap, nwords, scales_ap, sctag, ldtag, store_fn):
        p = tcount[0] % 2
        tcount[0] += 1
        d = DQ[p]
        nb = 4 * nwords
        ne = 8 * nwords
        by = d["by"][:, 0:nb]
        t1 = d["t1"][:, 0:nb]
        t2 = d["t2"][:, 0:nb]
        Wu = d["W"].bitcast(U16)
        Gu = d["G"].bitcast(U16)
        Wb = d["W"][:, 0:ne]
        Gb = d["G"][:, 0:ne]
        Cb = d["C"][:, 0:ne]
        nby, nt1, nt2 = f"q{p}by", f"q{p}t1", f"q{p}t2"
        nW, nG, nC = f"q{p}W", f"q{p}G", f"q{p}C"
        OP("vector", lambda e: e.tensor_copy(by[:, :], words_ap.bitcast(U8)),
           r=[ldtag], w=[nby], tag="dq_by")
        for h in range(2):
            Wh = Wu[:, h * nb:h * nb + nb]
            Gh = Gu[:, h * nb:h * nb + nb]
            if h == 0:
                OP("vector", lambda e: e.tensor_scalar(
                    t1[:, :], by[:, :], 15, 6, ALU.bitwise_and,
                    ALU.logical_shift_left), r=[nby], w=[nt1], tag="dq_t1")
                OP("vector", lambda e: e.tensor_scalar(
                    t2[:, :], by[:, :], 8, 6, ALU.bitwise_and,
                    ALU.logical_shift_left), r=[nby], w=[nt2], tag="dq_t2")
            else:
                OP("vector", lambda e: e.tensor_scalar(
                    t1[:, :], by[:, :], 4, 6, ALU.logical_shift_right,
                    ALU.logical_shift_left), r=[nby], w=[nt1], tag="dq_t1")
                OP("vector", lambda e: e.tensor_scalar(
                    t2[:, :], by[:, :], 128, 2, ALU.bitwise_and,
                    ALU.logical_shift_left), r=[nby], w=[nt2], tag="dq_t2")
            OP("vector", lambda e, Wh=Wh: e.tensor_tensor(
                Wh[:, :], t1[:, :], t2[:, :], ALU.subtract),
               r=[nt1, nt2], w=[nW], tag="dq_sub")
            OP("vector", lambda e, Wh=Wh: e.tensor_scalar(
                Wh[:, :], Wh[:, :], 16128, None, ALU.add),
               r=[nW], w=[nW], tag="dq_add")
            OP("vector", lambda e, Gh=Gh: e.tensor_scalar(
                Gh[:, :], t2[:, :], 6, 16256, ALU.logical_shift_left,
                ALU.bitwise_or), r=[nt2], w=[nG], tag="dq_g")
        OP("scalar", lambda e: e.activation(
            Cb[:, :], Wb[:, :], AF.Relu, bias=cc(C_ONE), scale=cc(C_MIN1)),
           r=[nW, "cst"], w=[nC], tag="dq_relu")
        OP("vector", lambda e: e.tensor_tensor(
            Wb[:, :], Wb[:, :], Cb[:, :], ALU.subtract),
           r=[nW, nC], w=[nW], tag="dq_mag")
        OP("vector", lambda e: e.tensor_tensor(
            Wb[:, :], Wb[:, :], Gb[:, :], ALU.mult),
           r=[nW, nG], w=[nW], tag="dq_val")
        ng = ne // 128
        W3 = Wb.rearrange("p (hh c) -> p hh c", hh=2)
        for g in range(ng):
            OP("vector", lambda e, g=g: e.tensor_scalar(
                W3[:, :, 64 * g:64 * (g + 1)], W3[:, :, 64 * g:64 * (g + 1)],
                scales_ap[:, g:g + 1], None, ALU.mult),
                r=[sctag], w=[nW], lazy=(g < ng - 1), tag="dq_sc")
        store_fn(Wb, nW, p)

    def mk_gu_unit(c):
        sl = c % 2
        wsl = guld[:, 256 * sl:256 * sl + 256]

        def load():
            OP("sync", lambda e, c=c, wsl=wsl: e.dma_start(
                out=wsl[:, :], in_=gup_d[128 * c:128 * c + 128, :]),
                w=[f"gld{sl}"], dma=f"gw{sl}", tag="gu_ld")
            OP("sync", lambda e, c=c: e.dma_start(
                out=sc_guh[sl][:, :], in_=gus_d[:, 128 * c:128 * c + 128],
                transpose=True), w=[f"gsch{sl}"], dma=f"gs{sl}", tag="gu_scld")

        def unit():
            OP("vector", lambda e: e.tensor_copy(sc_gu[sl][:, :],
                                                 sc_guh[sl][:, :]),
               r=[f"gsch{sl}"], w=[f"gscf{sl}"], small=True, tag="gu_scvt")
            for half in range(2):
                wv = wsl[:, 128 * half:128 * (half + 1)]

                def store_gu(dense, dtag, p, c=c, half=half):
                    osub = c % 4
                    for g3 in range(2 * half, 2 * half + 2):
                        k = 4 + g3 % 2
                        for ecl in range(4):
                            el = (g3 - 2 * half) * 4 + ecl
                            OP("tensor", lambda e, el=el, ecl=ecl, k=k,
                               dense=dense: e.transpose(
                                   pbh[k][:, 128 * ecl:128 * ecl + 128],
                                   dense[:, 128 * el:128 * el + 128],
                                   ident[:]),
                               r=[dtag, "ident"], w=[f"ps{k}"],
                               lazy=(ecl < 3), tag="guT")
                        OP("scalar", lambda e, g3=g3, osub=osub, k=k: e.copy(
                            stgG5[:, g3, :, osub, :],
                            pbh[k][:, 0:512].rearrange("p (l f) -> p l f",
                                                       l=4)),
                            r=[f"ps{k}"], w=[f"sg{g3}_{osub}"], tag="guTev")
                    if c % 4 == 3 and half == 1:
                        colidx = c // 4
                        for g3 in range(4):
                            rb = (g3 * 22 + colidx) * 128
                            OP("scalar", lambda e, g3=g3, rb=rb: e.dma_start(
                                out=sguP_d[rb:rb + 128, :],
                                in_=stgG[:, 2048 * g3:2048 * g3 + 2048]),
                                r=[f"sg{g3}_0", f"sg{g3}_1", f"sg{g3}_2",
                                   f"sg{g3}_3"], dma=f"fg{g3}", tag="gu_st")

                dq_tile(wv, 128, sc_gu[sl][:, 8 * half:8 * half + 8],
                        f"gscf{sl}", f"gld{sl}", store_gu)
        return load, unit

    loadG, emitG = zip(*[mk_gu_unit(c) for c in range(2 * NI // 128)])

    def mk_dn_unit(c):
        sl = c % 2

        def unit():
            OP("sync", lambda e, c=c: e.dma_start(
                out=dnld[:, :], in_=dnp_d[128 * c:128 * c + 128, :]),
                w=["dld"], dma="gw0", tag="dn_ld")
            for tt in range(3):
                OP("sync", lambda e, c=c, tt=tt: e.dma_start(
                    out=sc_dnh[sl][:, 16 * tt:16 * tt + 16],
                    in_=dns_d[16 * tt:16 * tt + 16, 128 * c:128 * c + 128],
                    transpose=True), w=[f"dsch{sl}"], dma=f"gs{sl}",
                    tag="dn_scld")
            OP("vector", lambda e: e.tensor_copy(sc_dn[sl][:, :],
                                                 sc_dnh[sl][:, 0:44]),
               r=[f"dsch{sl}"], w=[f"dscf{sl}"], small=True, tag="dn_scvt")
            for bq in range(6):
                nwords = 128 if bq < 5 else 64
                wv = dnld[:, 128 * bq:128 * bq + nwords]

                def store_dn(dense, dtag, p, c=c, bq=bq, nwords=nwords):
                    for g in range(nwords // 64):
                        k = 4 + (bq * 2 + g) % 2
                        for ecl in range(4):
                            el = 4 * g + ecl
                            OP("tensor", lambda e, el=el, ecl=ecl, k=k,
                               dense=dense: e.transpose(
                                   pbh[k][:, 128 * ecl:128 * ecl + 128],
                                   dense[:, 128 * el:128 * el + 128],
                                   ident[:]),
                               r=[dtag, "ident"], w=[f"ps{k}"],
                               lazy=(ecl < 3), tag="dnT")
                        OP("scalar", lambda e, p=p, g=g, k=k: e.copy(
                            stgD[:, 1024 * p + 512 * g:1024 * p + 512 * g + 512],
                            pbh[k][:, 0:512]),
                            r=[f"ps{k}"], w=[f"sd{p}_{g}"], tag="dnTev")
                        for i in range(4):
                            icg = 8 * bq + 4 * g + i
                            rb = ((icg // 4) * 4 + c // 4) * 128
                            co = 512 * (icg % 4) + 128 * (c % 4)
                            OP("scalar", lambda e, rb=rb, co=co, p=p, g=g, i=i:
                               e.dma_start(
                                   out=sdnP_d[rb:rb + 128, co:co + 128],
                                   in_=stgD[:, 1024 * p + 512 * g + 128 * i:
                                            1024 * p + 512 * g + 128 * i + 128]),
                               r=[f"sd{p}_{g}"], dma=f"fd{icg % 4}",
                               tag="dn_st")

                dq_tile(wv, nwords, sc_dn[sl][:, 8 * bq:8 * bq + nwords // 16],
                        f"dscf{sl}", "dld", store_dn)
        return unit

    emitD = [mk_dn_unit(c) for c in range(E // 128)]

    na, ngu = len(emitA), len(emitG)
    done_a = la = lg = 0
    for i, gfn in enumerate(emitG):
        while lg <= min(i + 1, ngu - 1):
            loadG[lg]()
            lg += 1
        gfn()
        want = ((i + 1) * na) // ngu
        while la < min(want + 3, na):
            loadA[la]()
            la += 1
        while done_a < want:
            emitA[done_a]()
            done_a += 1
    while done_a < na:
        while la < min(done_a + 4, na):
            loadA[la]()
            la += 1
        emitA[done_a]()
        done_a += 1
    for dfn in emitD:
        dfn()

    sch.barrier()

    wcount = [0]

    def stream_half(srcP, nf, kq, colidx, j):
        i = wcount[0] % 4
        wcount[0] += 1
        half = wb[:, 1024 * i:1024 * i + 1024]
        rb = (kq * nf + colidx) * 128
        eng = "sync" if i % 2 == 0 else "gpsimd"
        OP(eng, lambda e, half=half, rb=rb, j=j, srcP=srcP: e.dma_start(
            out=half, in_=srcP[rb:rb + 128, 1024 * j:1024 * j + 1024]),
            w=[f"wq{i}"], dma=f"wq{i}", tag="wstream")
        return half, f"wq{i}"

    WBALL = ["wq0", "wq1", "wq2", "wq3"]

    def transpose_into_xfm(r, src_bf, srctag="xbf"):
        x5 = src_bf.rearrange("p (b J t h) -> p b h J t", b=2, J=128, t=4, h=2)
        for ec in range(NEC):
            b_, hh, q = ec // 8, (ec % 8) // 4, ec % 4
            k = 6 + (ec % 2)
            OP("tensor", lambda e, b_=b_, hh=hh, q=q, k=k, x5=x5:
               e.transpose(pbh[k][:, 0:128],
                           x5[:, b_, hh, 32 * q:32 * (q + 1), :], ident[:]),
               r=[srctag, "ident"], w=[f"ps{k}"], tag="xT")
            OP("scalar", lambda e, ec=ec, r=r, k=k:
               e.copy(xfm[:, TB * ec + 128 * r:TB * ec + 128 * r + 128],
                      pbh[k][:, 0:128]),
               r=[f"ps{k}"], w=["xfm"], tag="xTev")

    kf = (kf0, kf1)
    vt = (vt0, vt1)

    for blk in range(nblk):
        par = blk % 2
        kfC, kfP = kf[par], kf[1 - par]
        vtC, vtP = vt[par], vt[1 - par]
        kfCt, kfPt = f"kf{par}", f"kf{1 - par}"
        vtCt, vtPt = f"vt{par}", f"vt{1 - par}"
        tb0 = blk * TB

        sch.claim("vector", nop, ["h", "wr0", "wr1", "wr2", "wr3"] + AR1_ATTN,
                  tag="claimA")

        for r in range(NTT):
            Br = B[:, E * r:E * (r + 1)]
            OP("scalar", lambda e, r=r, Br=Br, tb0=tb0: e.dma_start(
                out=Br[:, :], in_=x_d[tb0 + 128 * r:tb0 + 128 * r + 128, :]),
                w=[f"B{r}"], dma=f"xl{r}", tag="xload")
            OP("vector", lambda e, Br=Br: e.tensor_copy(xbf[:], Br[:, :]),
               r=[f"B{r}"], w=["xbf"], tag="xcast")
            transpose_into_xfm(r, xbf)

        if dbg_stop == "p1" and blk == nblk - 1:
            break
        for fg in range(12):
            bks = (0, 1, 2, 3) if fg % 2 == 0 else (4, 5, 6, 7)
            for kq in range(4):
                for j in range(2):
                    qslot, stag = stream_half(sqP_d, 12, kq, fg, j)
                    for kcl in range(2 * j, 2 * j + 2):
                        kc = 4 * kq + kcl
                        slot = qslot[:, 512 * (kcl - 2 * j):512 * (kcl - 2 * j) + 512]
                        for m in range(4):
                            first, last = kc == 0, kc == NEC - 1
                            bk = bks[m]
                            if fg < 8:
                                OP("tensor", lambda e, slot=slot, kc=kc, m=m, bk=bk,
                                   first=first, last=last: e.matmul(
                                       pb[bk][:, :], slot[:, 128 * m:128 * (m + 1)],
                                       xfm[:, TB * kc:TB * (kc + 1)],
                                       start=first, stop=last),
                                   r=[stag, "xfm"], w=[f"ps{bk}"], lazy=(m < 3),
                                   tag="qk_mm")
                            else:
                                OP("tensor", lambda e, slot=slot, kc=kc, m=m, bk=bk,
                                   first=first, last=last: e.matmul(
                                       pb[bk][:, :],
                                       xfm[:, TB * kc + 128 * m:TB * kc + 128 * (m + 1)],
                                       slot[:, :], start=first, stop=last),
                                   r=[stag, "xfm"], w=[f"ps{bk}"], lazy=(m < 3),
                                   tag="v_mm")
            for m in range(4):
                bk = bks[m]
                if fg < 4:
                    hch = 4 * fg + m
                    OP("scalar", lambda e, hch=hch, bk=bk:
                       e.copy(qf[:, TB * hch:TB * (hch + 1)], pb[bk][:, :]),
                       r=[f"ps{bk}"], w=["qf"], tag="q_ev")
                elif fg < 8:
                    hch = 4 * (fg - 4) + m
                    OP("scalar", lambda e, hch=hch, bk=bk, kfC=kfC:
                       e.copy(kfC[:, TB * hch:TB * (hch + 1)], pb[bk][:, :]),
                       r=[f"ps{bk}"], w=[kfCt], tag="k_ev")
                else:
                    vg = fg - 8
                    OP("scalar", lambda e, vg=vg, m=m, bk=bk, vtC=vtC:
                       e.copy(vtC[:, E * m + 512 * vg:E * m + 512 * (vg + 1)],
                              pb[bk][:, :]),
                       r=[f"ps{bk}"], w=[vtCt], tag="v_ev")

        if dbg_stop == "p2" and blk == nblk - 1:
            break
        c_lo = 4 if blk == 0 else 0
        for hh in range(NH):
            pvb = 4 + (hh % 2)
            for half in range(c_lo // 4, 2):
                for cl in range(4):
                    c = 4 * half + cl
                    if c < 4:
                        ksrc, ktag = kfP, kfPt
                        kcol = TB * hh + 128 * c
                    else:
                        ksrc, ktag = kfC, kfCt
                        kcol = TB * hh + 128 * (c - 4)
                    ql = max(0, 128 * (c - 4))
                    qh = min(512, 128 * (c + 1))
                    OP("tensor", lambda e, ksrc=ksrc, kcol=kcol, cl=cl, hh=hh,
                       ql=ql, qh=qh:
                       e.matmul(pb[cl][:, ql:qh], ksrc[:, kcol:kcol + 128],
                                qf[:, TB * hh + ql:TB * hh + qh], start=True,
                                stop=True),
                       r=[ktag, "qf"], w=[f"ps{cl}"], lazy=(cl < 3),
                       tag="score")
                OP("scalar", lambda e: e.activation(
                    es[:, :], pq[:, :], AF.Exp, bias=cc(C_ZERO),
                    scale=cc(C_EXPSC)),
                    r=["ps0", "ps1", "ps2", "ps3", "cst"],
                    w=["esA", "es0", "es1", "es2", "es3"], tag="exp")
                for cl in range(4):
                    c = 4 * half + cl
                    if c < 4:
                        msrc, mt = mkp, "mkp"
                        mcol = 512 - 128 * c
                        vsrc, vtag = vtP, vtPt
                        vcol = E * c + 128 * hh
                    else:
                        msrc, mt = mko, "mko"
                        mcol = 512 - 128 * (c - 4)
                        vsrc, vtag = vtC, vtCt
                        vcol = E * (c - 4) + 128 * hh
                    OP("vector", lambda e, cl=cl, msrc=msrc, mcol=mcol:
                       e.tensor_tensor(es[:, 512 * cl:512 * cl + 512],
                                       es[:, 512 * cl:512 * cl + 512],
                                       msrc[:, mcol:mcol + 512], ALU.mult),
                       r=["esA", mt], w=[f"es{cl}"], tag="mask")
                    OP("tensor", lambda e, cl=cl, c=c, c_lo=c_lo: e.matmul(
                        pb[6][0:1, :], onescol[:, 0:1],
                        es[:, 512 * cl:512 * cl + 512],
                        start=(c == c_lo), stop=(c == 7)),
                        r=[f"es{cl}", "ones"], w=["ps6"], lazy=(c < 7),
                        tag="summ")
                    OP("tensor", lambda e, vsrc=vsrc, vcol=vcol, cl=cl, c=c,
                       c_lo=c_lo, pvb=pvb:
                       e.matmul(pb[pvb][:, :], vsrc[:, vcol:vcol + 128],
                                es[:, 512 * cl:512 * cl + 512],
                                start=(c == c_lo), stop=(c == 7)),
                       r=[vtag, f"es{cl}"], w=[f"ps{pvb}"], lazy=(c < 7),
                       tag="pv")
            OP("scalar", lambda e: e.copy(srow[:, :], pb[6][0:1, :]),
               r=["ps6"], w=["srow"], small=True, tag="sum_ev")
            OP("vector", lambda e: e.tensor_copy(rh[:, :], srow[:, :]),
               r=["srow"], w=["rhi"], small=True, tag="rhi")
            OP("vector", lambda e: e.scalar_tensor_tensor(
                rl[:, :], rh[:, :], -1.0, srow[:, :], ALU.mult, ALU.add),
                r=["rhi", "srow"], w=["rlo"], small=True, tag="rlo")
            OP("tensor", lambda e: e.matmul(pb[7][:, :], onesrow[:, :],
                                            rh[:, :], start=True, stop=False),
               r=["rhi", "ones"], w=["ps7"], lazy=True, tag="bch")
            OP("tensor", lambda e: e.matmul(pb[7][:, :], onesrow[:, :],
                                            rl[:, :], start=False, stop=True),
               r=["rlo", "ones"], w=["ps7"], tag="bcl")
            OP("vector", lambda e: e.reciprocal(recb[:, :], pb[7][:, :]),
               r=["ps7"], w=["rec"], tag="recip")
            OP("vector", lambda e, hh=hh, pvb=pvb: e.scalar_tensor_tensor(
                af[:, TB * hh:TB * (hh + 1)], pb[pvb][:, :], 1.0, recb[:, :],
                ALU.bypass, ALU.mult),
                r=[f"ps{pvb}", "rec"], w=["af"], tag="attn_ev")

        if dbg_stop == "p3" and blk == nblk - 1:
            break
        for og in range(4):
            bks = (0, 1, 2, 3) if og % 2 == 0 else (4, 5, 6, 7)
            for kq in range(4):
                for j in range(2):
                    qslot, stag = stream_half(soP_d, 4, kq, og, j)
                    for kcl in range(2 * j, 2 * j + 2):
                        kc = 4 * kq + kcl
                        slot = qslot[:, 512 * (kcl - 2 * j):512 * (kcl - 2 * j) + 512]
                        for m in range(4):
                            bk = bks[m]
                            OP("tensor", lambda e, slot=slot, kc=kc, m=m, bk=bk:
                               e.matmul(pb[bk][:, :],
                                        af[:, TB * kc + 128 * m:TB * kc + 128 * (m + 1)],
                                        slot[:, :], start=(kc == 0),
                                        stop=(kc == NEC - 1)),
                               r=[stag, "af"], w=[f"ps{bk}"], lazy=(m < 3),
                               tag="pj_mm")
            for r in range(4):
                bk = bks[r]
                OP("vector", lambda e, r=r, og=og, bk=bk: e.scalar_tensor_tensor(
                    A[:, E * r + 512 * og:E * r + 512 * (og + 1)], pb[bk][:, :],
                    1.0, B[:, E * r + 512 * og:E * r + 512 * (og + 1)],
                    ALU.bypass, ALU.add),
                    r=[f"ps{bk}", f"B{r}"], w=[f"A{r}"], tag="resid1")
        if dbg_stop == "p4a" and blk == nblk - 1:
            break
        def mk_ln1(r):
            Ar = A[:, E * r:E * (r + 1)]
            Br = B[:, E * r:E * (r + 1)]
            sc = 6 * r
            cb, cbt = (xbf, "xbf") if r % 2 == 0 else (xbf2, "xb2")

            def st(i):
                return stats[:, sc + i:sc + i + 1]

            def nm(i):
                return f"st{i}_{r}"
            return [
                lambda: OP("vector", lambda e: e.reduce_sum(st(0), Ar[:, :],
                                                            axis=AXX),
                           r=[f"A{r}"], w=[nm(0)], small=True, tag="l1sum"),
                lambda: OP("vector", lambda e: e.tensor_scalar(
                    st(1), st(0), -1.0 / E, None, ALU.mult),
                    r=[nm(0)], w=[nm(1)], small=True, tag="l1mean"),
                lambda: OP("scalar", lambda e: e.activation(
                    Br[:, :], Ar[:, :], AF.Identity, bias=st(1), scale=cc(C_ONE)),
                    r=[f"A{r}", nm(1), "cst"], w=[f"B{r}"], tag="l1ctr"),
                lambda: OP("scalar", lambda e: e.activation(
                    xbf[:], Br[:, :], AF.Square, bias=cc(C_ZERO),
                    scale=cc(C_ONE), accum_out=st(2)),
                    r=[f"B{r}", "cst"], w=[nm(2), "xbf"], tag="l1sq"),
                lambda: OP("vector", lambda e: e.tensor_scalar(
                    st(3), st(2), 1.0 / E, LN_EPS, ALU.mult, ALU.add),
                    r=[nm(2)], w=[nm(3)], small=True, tag="l1var"),
                lambda: OP("scalar", lambda e: e.activation(
                    st(4), st(3), AF.Sqrt, bias=cc(C_ZERO), scale=cc(C_ONE)),
                    r=[nm(3), "cst"], w=[nm(4)], small=True, tag="l1std"),
                lambda: OP("vector", lambda e: e.reciprocal(st(5), st(4)),
                           r=[nm(4)], w=[nm(5)], small=True, tag="l1rstd"),
                lambda: OP("vector", lambda e: e.tensor_scalar(
                    Ar[:, :], Br[:, :], st(5), None, ALU.mult),
                    r=[f"B{r}", nm(5)], w=[f"A{r}"], tag="l1app"),
                lambda: (OP("vector", lambda e: e.tensor_copy(cb[:], Ar[:, :]),
                            r=[f"A{r}"], w=[cbt], tag="h1cast"),
                         transpose_into_xfm(r, cb, cbt)),
            ]

        ch1 = [mk_ln1(r) for r in range(NTT)]
        for si in range(len(ch1[0])):
            for r in range(NTT):
                ch1[r][si]()

        if dbg_stop == "p5" and blk == nblk - 1:
            break
        sch.claim("vector", nop, AR1_ATTN + ["h", "wr0", "wr1", "wr2", "wr3"],
                  tag="claimH")

        for g2 in range(NI // 512):
            for side in range(2):
                bks = (0, 1, 2, 3) if side == 0 else (4, 5, 6, 7)
                for kq in range(4):
                    for j in range(2):
                        qslot, stag = stream_half(sguP_d, 22, kq,
                                                  11 * side + g2, j)
                        for kcl in range(2 * j, 2 * j + 2):
                            kc = 4 * kq + kcl
                            slot = qslot[:, 512 * (kcl - 2 * j):512 * (kcl - 2 * j) + 512]
                            for m in range(4):
                                OP("tensor", lambda e, slot=slot, kc=kc, m=m,
                                   bks=bks: e.matmul(
                                       pb[bks[m]][:, :],
                                       slot[:, 128 * m:128 * (m + 1)],
                                       xfm[:, TB * kc:TB * (kc + 1)],
                                       start=(kc == 0), stop=(kc == NEC - 1)),
                                   r=[stag, "xfm"], w=[f"ps{bks[m]}"],
                                   lazy=(m < 3), tag="gu_mm")
            for m in range(4):
                ic = 4 * g2 + m
                OP("scalar", lambda e, m=m: e.activation(
                    gs[:], pb[m][:, :], AF.Silu, bias=cc(C_ZERO),
                    scale=cc(C_ONE)),
                    r=[f"ps{m}", "cst"], w=["gs"], tag="silu")
                OP("vector", lambda e, ic=ic, m=m: e.tensor_tensor(
                    h_fm[:, TB * ic:TB * (ic + 1)], gs[:], pb[4 + m][:, :],
                    ALU.mult), r=["gs", f"ps{4 + m}"], w=["h"], tag="silmul")

        if dbg_stop == "p6" and blk == nblk - 1:
            break
        for og in range(4):
            bks = (0, 1, 2, 3) if og % 2 == 0 else (4, 5, 6, 7)
            for icq in range(NIC // 4):
                for j in range(2):
                    qslot, stag = stream_half(sdnP_d, 4, icq, og, j)
                    for icl in range(2 * j, 2 * j + 2):
                        ic = 4 * icq + icl
                        slot = qslot[:, 512 * (icl - 2 * j):512 * (icl - 2 * j) + 512]
                        for r in range(4):
                            bk = bks[r]
                            OP("tensor", lambda e, slot=slot, ic=ic, r=r, bk=bk:
                               e.matmul(pb[bk][:, :],
                                        h_fm[:, TB * ic + 128 * r:TB * ic + 128 * (r + 1)],
                                        slot[:, :], start=(ic == 0),
                                        stop=(ic == NIC - 1)),
                               r=[stag, "h"], w=[f"ps{bk}"], lazy=(r < 3),
                               tag="dn_mm")
            for r in range(4):
                bk = bks[r]
                OP("scalar", lambda e, r=r, og=og, bk=bk: e.copy(
                    B[:, E * r + 512 * og:E * r + 512 * (og + 1)], pb[bk][:, :]),
                    r=[f"ps{bk}"], w=[f"B{r}"], tag="y_ev")
        if dbg_stop == "p7a" and blk == nblk - 1:
            break
        def mk_ln2(r, tb0=tb0):
            Ar = A[:, E * r:E * (r + 1)]
            Br = B[:, E * r:E * (r + 1)]
            Wr = wrF[r]
            wn = f"wr{r}"
            dum = ar1[:, 4096 * r:4096 * r + 2048]
            sc = 24 + 10 * r

            def sl(i):
                return stats[:, sc + i:sc + i + 1]

            def sn(i):
                return f"su{i}_{r}"
            return [
                lambda: OP("scalar", lambda e: e.activation(
                    dum, Br[:, :], AF.Square, bias=cc(C_ZERO),
                    scale=cc(C_ONE), accum_out=sl(0)),
                    r=[f"B{r}", "cst"], w=[sn(0), wn], tag="rsq"),
                lambda: OP("vector", lambda e: e.tensor_scalar(
                    sl(1), sl(0), 1.0 / E, RMS_EPS, ALU.mult, ALU.add),
                    r=[sn(0)], w=[sn(1)], small=True, tag="rms"),
                lambda: OP("scalar", lambda e: e.activation(
                    sl(2), sl(1), AF.Sqrt, bias=cc(C_ZERO), scale=cc(C_ONE)),
                    r=[sn(1), "cst"], w=[sn(2)], small=True, tag="rsqt"),
                lambda: OP("vector", lambda e: e.reciprocal(sl(3), sl(2)),
                           r=[sn(2)], w=[sn(3)], small=True, tag="rinv"),
                lambda: OP("vector", lambda e: e.scalar_tensor_tensor(
                    Wr[:, :], Br[:, :], sl(3), Ar[:, :], ALU.mult, ALU.add),
                    r=[f"B{r}", sn(3), f"A{r}"], w=[wn], tag="resid2"),
                lambda: OP("vector", lambda e: e.reduce_sum(
                    sl(4), Wr[:, :], axis=AXX),
                    r=[wn], w=[sn(4)], small=True, tag="l2sum"),
                lambda: OP("vector", lambda e: e.tensor_scalar(
                    sl(5), sl(4), -1.0 / E, None, ALU.mult),
                    r=[sn(4)], w=[sn(5)], small=True, tag="l2mean"),
                lambda: OP("scalar", lambda e: e.activation(
                    Ar[:, :], Wr[:, :], AF.Identity, bias=sl(5), scale=cc(C_ONE)),
                    r=[wn, sn(5), "cst"], w=[f"A{r}"], tag="l2ctr"),
                lambda: OP("scalar", lambda e: e.activation(
                    dum, Ar[:, :], AF.Square, bias=cc(C_ZERO),
                    scale=cc(C_ONE), accum_out=sl(6)),
                    r=[f"A{r}", "cst"], w=[sn(6), wn], tag="l2sq"),
                lambda: OP("vector", lambda e: e.tensor_scalar(
                    sl(7), sl(6), 1.0 / E, LN_EPS, ALU.mult, ALU.add),
                    r=[sn(6)], w=[sn(7)], small=True, tag="l2var"),
                lambda: OP("scalar", lambda e: e.activation(
                    sl(8), sl(7), AF.Sqrt, bias=cc(C_ZERO), scale=cc(C_ONE)),
                    r=[sn(7), "cst"], w=[sn(8)], small=True, tag="l2std"),
                lambda: OP("vector", lambda e: e.reciprocal(sl(9), sl(8)),
                           r=[sn(8)], w=[sn(9)], small=True, tag="l2rstd"),
                lambda: OP("vector", lambda e: e.tensor_scalar(
                    Br[:, :], Ar[:, :], sl(9), None, ALU.mult),
                    r=[f"A{r}", sn(9)], w=[f"B{r}"], tag="l2app"),
                lambda: OP("scalar", lambda e: e.dma_start(
                    out=y_d[tb0 + 128 * r:tb0 + 128 * r + 128, :],
                    in_=Br[:, :]),
                    r=[f"B{r}"], dma=f"ys{r}", tag="ystore"),
            ]

        ch2 = [mk_ln2(r) for r in range(NTT)]
        for si in range(len(ch2[0])):
            for r in range(NTT):
                ch2[r][si]()

    sch.barrier()
    tapmap = dict(A=A, B=B, xfm=xfm, qf=qf, es=es, af=af, kf0=kf0, kf1=kf1,
                  vt0=vt0, vt1=vt1, h=ar1, wb=wb, xbf=xbf, stats=stats,
                  sdn=sdnP_d, sgu=sguP_d, sq=sqP_d)
    for (name, ap_out) in tap_outs:
        key = name.split("__")[1]
        sap = tapmap[key]
        n_r = ap_out.shape[0]
        OP("gpsimd", lambda e, ap_out=ap_out, sap=sap, n_r=n_r: e.dma_start(
            out=ap_out, in_=sap[0:n_r, :]), r=[], dma="tap", tag="tap")

    sch.barrier()
    print(f"[build] ops={len(sch.ops)} instrs~={sch.n_instr} "
          f"sbuf_free={nc.sbuf_bytes_remaining} psum_free={nc.psum_banks_remaining}")
    blk_cm = nc.Block()
    block = blk_cm.__enter__()
    sch.emit(block)
    blk_cm.__exit__(None, None, None)
    st.close()
    return nc


def _consts_np():
    ident = np.eye(128, dtype=ml_dtypes.bfloat16)
    onesc = np.ones((128, 1), dtype=ml_dtypes.bfloat16)
    onesr = np.ones((1, 128), dtype=ml_dtypes.bfloat16)
    r = np.arange(128)[:, None]
    u = np.arange(1024)[None, :]
    MP = (r >= (u - 512)).astype(ml_dtypes.bfloat16)
    MO = (r <= (u - 512)).astype(ml_dtypes.bfloat16)
    mask = np.concatenate([MP, MO], axis=1)
    cst = np.tile(np.array(CONSTS, dtype=np.float32), (128, 1))
    return ident, onesc, onesr, mask, cst


def _pi(n):
    out = np.zeros(n, dtype=np.int64)
    b0 = 0
    while b0 < n:
        w = min(1024, n - b0)
        e = np.arange(w)
        j, kn = e // 8, e % 8
        t, hbit = kn // 2, kn % 2
        ep = (w // 2) * hbit + 4 * j + t
        out[b0 + ep] = b0 + e
        b0 += w
    return out


def make_in_map(xb, wqkv, wout, gup, gus, dnp, dns):
    ident, onesc, onesr, mask, cst = _consts_np()
    pi_e = _pi(E)
    pi_i = _pi(NI)
    rows = np.concatenate([pi_i, NI + pi_i])
    return dict(x=np.ascontiguousarray(xb),
                wqkv=np.ascontiguousarray(np.asarray(wqkv)[:, pi_e]),
                wout=np.asarray(wout),
                gup=np.ascontiguousarray(np.asarray(gup)[rows]),
                gus=np.ascontiguousarray(np.asarray(gus)[:, rows]),
                dnp=np.asarray(dnp),
                dns=np.pad(np.asarray(dns), ((0, 4), (0, 0))),
                identin=ident, onescin=onesc,
                onesrin=onesr, maskin=mask, cstin=cst)


def kernel(L0_attn_x, L0_attn_w_qkv, L0_attn_w_out, L0_ffn_gate_up_packed,
           L0_ffn_gate_up_scales, L0_ffn_down_packed, L0_ffn_down_scales):
    nc = build_program(nblk=8)
    in_maps = [make_in_map(L0_attn_x[b], L0_attn_w_qkv, L0_attn_w_out,
                           L0_ffn_gate_up_packed, L0_ffn_gate_up_scales,
                           L0_ffn_down_packed, L0_ffn_down_scales)
               for b in range(8)]
    res = run_bass_kernel_spmd(nc, in_maps, core_ids=list(range(8)))
    return np.stack([np.asarray(r["y"], dtype=np.float32)
                     for r in res.results])


# version 17 (final)
# speedup vs baseline: 1.0537x; 1.0295x over previous
import numpy as np
import ml_dtypes
import contextlib
import concourse.bass as bass
import concourse.mybir as mybir
from concourse.bass_utils import run_bass_kernel_spmd

F32 = mybir.dt.float32
BF16 = mybir.dt.bfloat16
F16 = mybir.dt.float16
I32 = mybir.dt.int32
U8 = mybir.dt.uint8
AF = mybir.ActivationFunctionType
ALU = mybir.AluOpType
AXX = mybir.AxisListType.X

S = 4096
E = 2048
NH = 16
NI = 5632
TB = 512
NTT = 4
NEC = 16
NIC = 44
INV_SQRT_HD = 1.0 / float(np.sqrt(128.0))
LN_EPS = 1e-5
RMS_EPS = 1e-6

C_ONE, C_EXPSC, C_M8, C_M10, C_SGS, C_SGB, C_ZERO, C_MIN1 = range(8)
CONSTS = [1.0, INV_SQRT_HD, -8.0, -10.0, -1.0 / 16.0, 7.5 / 16.0, 0.0,
          -1.0, 0, 0, 0, 0, 0, 0, 0, 0]

ENGS = ("sync", "gpsimd", "vector", "scalar", "tensor")


class Buf:
    __slots__ = ("name", "w", "readers", "w_small")

    def __init__(self, name):
        self.name = name
        self.w = None
        self.w_small = False
        self.readers = []


class Sem:
    __slots__ = ("name", "h", "n", "lazy")

    def __init__(self, name, h):
        self.name = name
        self.h = h
        self.n = 0
        self.lazy = False


class Sched:
    def __init__(self, nc=None, st=None):
        self.ops = []
        self.sems = {}
        self.water = {e: {} for e in ENGS}
        self.bufs = {}
        self.n_instr = 0
        self.nc = nc
        self.st = st

    def gb(self, name):
        if name not in self.bufs:
            self.bufs[name] = Buf(name)
        return self.bufs[name]

    def sem_for(self, eng, dma):
        if dma:
            key = "d_" + dma
            if key not in self.sems:
                h = self.st.enter_context(self.nc.semaphore(key))
                self.sems[key] = Sem(key, h)
            return self.sems[key]
        key = {"vector": "eV", "scalar": "eA", "tensor": "eT",
               "gpsimd": "eP"}[eng]
        return self.sems[key]

    def op(self, eng, emit, r=(), w=(), dma=False, lazy=False, small=False,
           tag=""):
        sem = self.sem_for(eng, dma)
        amt = 16 if dma else 1
        reads = [self.gb(x) for x in r]
        writes = [self.gb(x) for x in w]
        deps = []
        for b in reads:
            if b.w is not None:
                deps.append((b.w[0], b.w[1], b.w_small))
        for b in writes:
            if b.w is not None:
                deps.append((b.w[0], b.w[1], False))
            deps.extend((s_, v_, False) for (s_, v_) in b.readers)
        deps = [(s, s.n if s.name.startswith("d_") else v, f)
                for (s, v, f) in deps]
        need = {}
        own = None if dma else sem
        for (s, v, force) in deps:
            if s is own and not force:
                continue
            if need.get(s, -1) < v:
                need[s] = v
        wm = self.water[eng]
        waits = []
        for s, v in sorted(need.items(), key=lambda kv: kv[0].name):
            if wm.get(s, -1) >= v:
                continue
            wm[s] = v
            waits.append((s, v))
        if lazy:
            assert not dma
            val = sem.n + 1
            sem.lazy = True
        else:
            sem.n += amt
            val = sem.n
            sem.lazy = False
        for b in writes:
            b.w = (sem, val)
            b.w_small = small
            b.readers = []
        for b in reads:
            b.readers.append((sem, val))
        self.ops.append((eng, waits, emit, None if lazy else sem, amt, tag))
        self.n_instr += 1 + len(waits)

    def claim(self, eng, emitnop, names, tag="claim"):
        self.op(eng, emitnop, r=(), w=names, tag=tag)

    def barrier(self):
        for s in self.sems.values():
            assert not s.lazy, f"lazy pending on {s.name} at barrier"
        cur = [(s, s.n) for s in self.sems.values() if s.n > 0]
        for e in ENGS:
            wm = self.water[e]
            todo = [(s, v) for (s, v) in cur if wm.get(s, -1) < v]
            for (s, v) in todo:
                wm[s] = v
            if todo:
                self.ops.append((e, todo, None, None, 0, "barrier"))
                self.n_instr += len(todo)

    def emit(self, block):
        def mk(name):
            def body(eng):
                for (e, waits, emit, sem, amt, tag) in self.ops:
                    if e != name:
                        continue
                    for (s, v) in waits:
                        eng.wait_ge(s.h, v)
                    if emit is None:
                        continue
                    ins = emit(eng)
                    if sem is not None:
                        ins.then_inc(sem.h, amt)
            return body
        block.sync(mk("sync"))
        block.gpsimd(mk("gpsimd"))
        block.vector(mk("vector"))
        block.scalar(mk("scalar"))
        block.tensor(mk("tensor"))


def build_program(nblk=8, taps=None, dbg_stop=None, rd=True):
    nc = bass.Bass("TRN2", target_bir_lowering=False,
                   detect_race_conditions=rd)
    st = contextlib.ExitStack()

    x_d = nc.dram_tensor("x", [S, E], F32, kind="ExternalInput").ap()
    wqkv_d = nc.dram_tensor("wqkv", [3 * E, E], F32, kind="ExternalInput").ap()
    wout_d = nc.dram_tensor("wout", [E, E], F32, kind="ExternalInput").ap()
    gup_d = nc.dram_tensor("gup", [2 * NI, E // 8], I32, kind="ExternalInput").ap()
    gus_d = nc.dram_tensor("gus", [16, 2 * NI], F16, kind="ExternalInput").ap()
    dnp_d = nc.dram_tensor("dnp", [E, NI // 8], I32, kind="ExternalInput").ap()
    dns_d = nc.dram_tensor("dns", [48, E], F16, kind="ExternalInput").ap()
    ident_d = nc.dram_tensor("identin", [128, 128], BF16, kind="ExternalInput").ap()
    onesc_d = nc.dram_tensor("onescin", [128, 1], BF16, kind="ExternalInput").ap()
    onesr_d = nc.dram_tensor("onesrin", [1, 128], BF16, kind="ExternalInput").ap()
    mask_d = nc.dram_tensor("maskin", [128, 2048], BF16, kind="ExternalInput").ap()
    cst_d = nc.dram_tensor("cstin", [128, 16], F32, kind="ExternalInput").ap()
    y_d = nc.dram_tensor("y", [S, E], F32, kind="ExternalOutput").ap()
    sqP_d = nc.dram_tensor("scr_qP", [4 * 12 * 128, 2048], BF16, kind="Internal").ap()
    soP_d = nc.dram_tensor("scr_oP", [4 * 4 * 128, 2048], BF16, kind="Internal").ap()
    sguP_d = nc.dram_tensor("scr_guP", [4 * 22 * 128, 2048], BF16, kind="Internal").ap()
    sdnP_d = nc.dram_tensor("scr_dnP", [11 * 4 * 128, 2048], BF16, kind="Internal").ap()

    tap_outs = []
    if taps:
        for (name, shape, dtype) in taps:
            tap_outs.append((name, nc.dram_tensor(name, shape, dtype,
                                                  kind="ExternalOutput").ap()))

    sb = lambda name, shape, dt: st.enter_context(nc.sbuf_tensor(name, shape, dt))
    A = sb("A", [128, 4 * E], F32)
    B = sb("B", [128, 4 * E], F32)
    xbf = sb("xbf", [128, E], BF16)
    xfm = sb("xfm", [128, NEC * TB], BF16)
    kf0 = sb("kf0", [128, NH * TB], BF16)
    kf1 = sb("kf1", [128, NH * TB], BF16)
    vt0 = sb("vt0", [128, NTT * E], BF16)
    vt1 = sb("vt1", [128, NTT * E], BF16)
    ar1 = sb("ar1", [128, NIC * TB], BF16)
    wb = sb("wb", [128, 8 * TB], BF16)
    mkp = sb("mkp", [128, 1024], BF16)
    mko = sb("mko", [128, 1024], BF16)
    gs = sb("gs", [128, TB], BF16)
    ident = sb("ident", [128, 128], BF16)
    onescol = sb("onescol", [128, 1], BF16)
    onesrow = sb("onesrow", [1, 128], BF16)
    cst = sb("cst", [128, 16], F32)
    stats = sb("stats", [128, 64], F32)

    qf = ar1[:, 0:NH * TB]
    es = ar1[:, NH * TB:NH * TB + 4 * TB]
    af = ar1[:, 20 * TB:36 * TB]
    recb = ar1[:, 36 * TB:38 * TB].bitcast(F32)
    rh = ar1[:, 38 * TB:39 * TB][0:1, 0:512]
    rl = ar1[:, 39 * TB:40 * TB][0:1, 0:512]
    srow = ar1[:, 40 * TB:42 * TB].bitcast(F32)[0:1, 0:512]
    h_fm = ar1
    wrF = [ar1[:, 4096 * r:4096 * (r + 1)].bitcast(F32) for r in range(4)]

    AR1_ATTN = ["qf", "esA", "es0", "es1", "es2", "es3", "af", "rec",
                "srow", "rhi", "rlo", "xb2"]
    xbf2 = ar1[:, 0:2048]

    pq = st.enter_context(nc.psum_tensor("pq", [128, 2048], F32))
    pb = [pq[:, 512 * i:512 * (i + 1)] for i in range(4)] + \
         [st.enter_context(nc.psum_tensor(f"pb{i}", [128, 512], F32))
          for i in range(4, 8)]
    pbh = {k: pb[k].bitcast(BF16) for k in (4, 5, 6, 7)}

    sch = Sched(nc, st)
    for sname in ("eV", "eA", "eT", "eP"):
        h = st.enter_context(nc.semaphore(sname))
        sch.sems[sname] = Sem(sname, h)

    OP = sch.op

    def cc(i):
        return cst[:, i:i + 1]

    nop = lambda e: e.engine_nop()

    OP("sync", lambda e: e.dma_start(out=ident[:], in_=ident_d), w=["ident"], dma="c")
    OP("sync", lambda e: e.dma_start(out=onescol[:], in_=onesc_d), w=["ones"], dma="c")
    OP("sync", lambda e: e.dma_start(out=onesrow[:], in_=onesr_d), w=["ones"], dma="c")
    OP("sync", lambda e: e.dma_start(out=mkp[:], in_=mask_d[:, 0:1024]), w=["mkp"], dma="c")
    OP("sync", lambda e: e.dma_start(out=mko[:], in_=mask_d[:, 1024:2048]), w=["mko"], dma="c")
    OP("sync", lambda e: e.dma_start(out=cst[:], in_=cst_d), w=["cst"], dma="c")

    stgA = xfm
    stgA5 = stgA.rearrange("p (g l o f) -> p g l o f", g=4, l=4, o=4)

    def tA_stage(dense, dense_tag, osub, tag):
        for g3 in range(4):
            k = 6 + g3 % 2
            for ecl in range(4):
                ec = 4 * g3 + ecl
                OP("tensor", lambda e, ec=ec, ecl=ecl, k=k, dense=dense:
                   e.transpose(pbh[k][:, 128 * ecl:128 * ecl + 128],
                               dense[:, 128 * ec:128 * ec + 128], ident[:]),
                   r=[dense_tag, "ident"], w=[f"ps{k}"], lazy=(ecl < 3),
                   tag=tag + "T")
            OP("scalar", lambda e, g3=g3, osub=osub, k=k:
               e.copy(stgA5[:, g3, :, osub, :],
                      pbh[k][:, 0:512].rearrange("p (l f) -> p l f", l=4)),
               r=[f"ps{k}"], w=[f"sa{g3}_{osub}"], tag=tag + "Tev")

    def tA_flush(dstP, nf, fg, tag):
        for g3 in range(4):
            rb = (g3 * nf + fg) * 128
            OP("scalar", lambda e, g3=g3, rb=rb, dstP=dstP: e.dma_start(
                out=dstP[rb:rb + 128, :],
                in_=stgA[:, 2048 * g3:2048 * g3 + 2048]),
                r=[f"sa{g3}_0", f"sa{g3}_1", f"sa{g3}_2", f"sa{g3}_3"],
                dma=f"fa{g3}", tag=tag + "st")

    w32 = (kf0.bitcast(F32)[:, 0:E], kf0.bitcast(F32)[:, E:2 * E],
           kf1.bitcast(F32)[:, 0:E], kf1.bitcast(F32)[:, E:2 * E])
    dstg = (vt0[:, 0:E], vt1[:, 0:E])

    emitA, loadA = [], []

    def mk_cast_unit(src_d, dstP, nf, c, j):
        sl4 = j % 4
        wsrc, wdst = w32[sl4], dstg[c % 2]
        ldb, stb = f"wld{sl4}", f"wst{c % 2}"

        def load():
            OP("gpsimd", lambda e, wsrc=wsrc, c=c, src_d=src_d: e.dma_start(
                out=wsrc[:, :], in_=src_d[128 * c:128 * c + 128, :]),
                w=[ldb], dma=f"wl{sl4}", tag="wc_ld")

        def unit():
            OP("vector", lambda e, wsrc=wsrc, wdst=wdst:
               e.tensor_copy(wdst[:, :], wsrc[:, :]),
               r=[ldb], w=[stb], tag="wc_cp")
            tA_stage(wdst, stb, c % 4, "wc")
            if c % 4 == 3:
                tA_flush(dstP, nf, c // 4, "wc")
        return load, unit

    _j = 0
    for c in range(3 * E // 128):
        ld, un = mk_cast_unit(wqkv_d, sqP_d, 12, c, _j)
        loadA.append(ld)
        emitA.append(un)
        _j += 1
    for c in range(E // 128):
        ld, un = mk_cast_unit(wout_d, soP_d, 4, c, _j)
        loadA.append(ld)
        emitA.append(un)
        _j += 1

    U16 = mybir.dt.uint16
    DQ = []
    for p in range(2):
        b0 = 4608 * p
        DQ.append(dict(
            by=ar1[:, b0:b0 + 512].bitcast(U16),
            t1=ar1[:, b0 + 512:b0 + 1024].bitcast(U16),
            t2=ar1[:, b0 + 1024:b0 + 1536].bitcast(U16),
            W=ar1[:, b0 + 1536:b0 + 2560],
            G=ar1[:, b0 + 2560:b0 + 3584],
            C=ar1[:, b0 + 3584:b0 + 4608],
        ))
    guld = ar1[:, 9216:10240].bitcast(I32)
    dnld = ar1[:, 10240:11648].bitcast(I32)
    stgG = ar1[:, 11648:19840]
    stgG5 = stgG.rearrange("p (g l o f) -> p g l o f", g=4, l=4, o=4)
    stgD = ar1[:, 19840:21888]
    sc_gu = [ar1[:, 21888 + 32 * p:21920 + 32 * p].bitcast(F32) for p in range(2)]
    sc_guh = [ar1[:, 21952 + 16 * p:21968 + 16 * p].bitcast(F16) for p in range(2)]
    sc_dn = [ar1[:, 21984 + 88 * p:22072 + 88 * p].bitcast(F32) for p in range(2)]
    sc_dnh = [ar1[:, 22160 + 48 * p:22208 + 48 * p].bitcast(F16) for p in range(2)]

    tcount = [0]

    def dq_tile(words_ap, nwords, scales_ap, sctag, ldtag, store_fn):
        p = tcount[0] % 2
        tcount[0] += 1
        d = DQ[p]
        nb = 4 * nwords
        ne = 8 * nwords
        by = d["by"][:, 0:nb]
        t1 = d["t1"][:, 0:nb]
        t2 = d["t2"][:, 0:nb]
        Wu = d["W"].bitcast(U16)
        Gu = d["G"].bitcast(U16)
        Wb = d["W"][:, 0:ne]
        Gb = d["G"][:, 0:ne]
        Cb = d["C"][:, 0:ne]
        nby, nt1, nt2 = f"q{p}by", f"q{p}t1", f"q{p}t2"
        nW, nG, nC = f"q{p}W", f"q{p}G", f"q{p}C"
        OP("vector", lambda e: e.tensor_copy(by[:, :], words_ap.bitcast(U8)),
           r=[ldtag], w=[nby], tag="dq_by")
        for h in range(2):
            Wh = Wu[:, h * nb:h * nb + nb]
            Gh = Gu[:, h * nb:h * nb + nb]
            if h == 0:
                OP("vector", lambda e: e.tensor_scalar(
                    t1[:, :], by[:, :], 15, 6, ALU.bitwise_and,
                    ALU.logical_shift_left), r=[nby], w=[nt1], tag="dq_t1")
                OP("vector", lambda e: e.tensor_scalar(
                    t2[:, :], by[:, :], 8, 6, ALU.bitwise_and,
                    ALU.logical_shift_left), r=[nby], w=[nt2], tag="dq_t2")
            else:
                OP("vector", lambda e: e.tensor_scalar(
                    t1[:, :], by[:, :], 4, 6, ALU.logical_shift_right,
                    ALU.logical_shift_left), r=[nby], w=[nt1], tag="dq_t1")
                OP("vector", lambda e: e.tensor_scalar(
                    t2[:, :], by[:, :], 128, 2, ALU.bitwise_and,
                    ALU.logical_shift_left), r=[nby], w=[nt2], tag="dq_t2")
            OP("vector", lambda e, Wh=Wh: e.tensor_tensor(
                Wh[:, :], t1[:, :], t2[:, :], ALU.subtract),
               r=[nt1, nt2], w=[nW], tag="dq_sub")
            OP("vector", lambda e, Wh=Wh: e.tensor_scalar(
                Wh[:, :], Wh[:, :], 16128, None, ALU.add),
               r=[nW], w=[nW], tag="dq_add")
            OP("vector", lambda e, Gh=Gh: e.tensor_scalar(
                Gh[:, :], t2[:, :], 6, 16256, ALU.logical_shift_left,
                ALU.bitwise_or), r=[nt2], w=[nG], tag="dq_g")
        OP("scalar", lambda e: e.activation(
            Cb[:, :], Wb[:, :], AF.Relu, bias=cc(C_ONE), scale=cc(C_MIN1)),
           r=[nW, "cst"], w=[nC], tag="dq_relu")
        OP("vector", lambda e: e.tensor_tensor(
            Wb[:, :], Wb[:, :], Cb[:, :], ALU.subtract),
           r=[nW, nC], w=[nW], tag="dq_mag")
        OP("vector", lambda e: e.tensor_tensor(
            Wb[:, :], Wb[:, :], Gb[:, :], ALU.mult),
           r=[nW, nG], w=[nW], tag="dq_val")
        ng = ne // 128
        W3 = Wb.rearrange("p (hh c) -> p hh c", hh=2)
        for g in range(ng):
            OP("vector", lambda e, g=g: e.tensor_scalar(
                W3[:, :, 64 * g:64 * (g + 1)], W3[:, :, 64 * g:64 * (g + 1)],
                scales_ap[:, g:g + 1], None, ALU.mult),
                r=[sctag], w=[nW], lazy=(g < ng - 1), tag="dq_sc")
        store_fn(Wb, nW, p)

    def mk_gu_unit(c):
        sl = c % 2
        wsl = guld[:, 256 * sl:256 * sl + 256]

        def load():
            OP("sync", lambda e, c=c, wsl=wsl: e.dma_start(
                out=wsl[:, :], in_=gup_d[128 * c:128 * c + 128, :]),
                w=[f"gld{sl}"], dma=f"gw{sl}", tag="gu_ld")
            OP("sync", lambda e, c=c: e.dma_start(
                out=sc_guh[sl][:, :], in_=gus_d[:, 128 * c:128 * c + 128],
                transpose=True), w=[f"gsch{sl}"], dma=f"gs{sl}", tag="gu_scld")

        def unit():
            OP("vector", lambda e: e.tensor_copy(sc_gu[sl][:, :],
                                                 sc_guh[sl][:, :]),
               r=[f"gsch{sl}"], w=[f"gscf{sl}"], small=True, tag="gu_scvt")
            for half in range(2):
                wv = wsl[:, 128 * half:128 * (half + 1)]

                def store_gu(dense, dtag, p, c=c, half=half):
                    osub = c % 4
                    for g3 in range(2 * half, 2 * half + 2):
                        k = 4 + g3 % 2
                        for ecl in range(4):
                            el = (g3 - 2 * half) * 4 + ecl
                            OP("tensor", lambda e, el=el, ecl=ecl, k=k,
                               dense=dense: e.transpose(
                                   pbh[k][:, 128 * ecl:128 * ecl + 128],
                                   dense[:, 128 * el:128 * el + 128],
                                   ident[:]),
                               r=[dtag, "ident"], w=[f"ps{k}"],
                               lazy=(ecl < 3), tag="guT")
                        OP("scalar", lambda e, g3=g3, osub=osub, k=k: e.copy(
                            stgG5[:, g3, :, osub, :],
                            pbh[k][:, 0:512].rearrange("p (l f) -> p l f",
                                                       l=4)),
                            r=[f"ps{k}"], w=[f"sg{g3}_{osub}"], tag="guTev")
                    if c % 4 == 3 and half == 1:
                        colidx = c // 4
                        for g3 in range(4):
                            rb = (g3 * 22 + colidx) * 128
                            OP("scalar", lambda e, g3=g3, rb=rb: e.dma_start(
                                out=sguP_d[rb:rb + 128, :],
                                in_=stgG[:, 2048 * g3:2048 * g3 + 2048]),
                                r=[f"sg{g3}_0", f"sg{g3}_1", f"sg{g3}_2",
                                   f"sg{g3}_3"], dma=f"fg{g3}", tag="gu_st")

                dq_tile(wv, 128, sc_gu[sl][:, 8 * half:8 * half + 8],
                        f"gscf{sl}", f"gld{sl}", store_gu)
        return load, unit

    loadG, emitG = zip(*[mk_gu_unit(c) for c in range(2 * NI // 128)])

    def mk_dn_unit(c):
        sl = c % 2

        def unit():
            OP("sync", lambda e, c=c: e.dma_start(
                out=dnld[:, :], in_=dnp_d[128 * c:128 * c + 128, :]),
                w=["dld"], dma="gw0", tag="dn_ld")
            for tt in range(3):
                OP("sync", lambda e, c=c, tt=tt: e.dma_start(
                    out=sc_dnh[sl][:, 16 * tt:16 * tt + 16],
                    in_=dns_d[16 * tt:16 * tt + 16, 128 * c:128 * c + 128],
                    transpose=True), w=[f"dsch{sl}"], dma=f"gs{sl}",
                    tag="dn_scld")
            OP("vector", lambda e: e.tensor_copy(sc_dn[sl][:, :],
                                                 sc_dnh[sl][:, 0:44]),
               r=[f"dsch{sl}"], w=[f"dscf{sl}"], small=True, tag="dn_scvt")
            for bq in range(6):
                nwords = 128 if bq < 5 else 64
                wv = dnld[:, 128 * bq:128 * bq + nwords]

                def store_dn(dense, dtag, p, c=c, bq=bq, nwords=nwords):
                    for g in range(nwords // 64):
                        k = 4 + (bq * 2 + g) % 2
                        for ecl in range(4):
                            el = 4 * g + ecl
                            OP("tensor", lambda e, el=el, ecl=ecl, k=k,
                               dense=dense: e.transpose(
                                   pbh[k][:, 128 * ecl:128 * ecl + 128],
                                   dense[:, 128 * el:128 * el + 128],
                                   ident[:]),
                               r=[dtag, "ident"], w=[f"ps{k}"],
                               lazy=(ecl < 3), tag="dnT")
                        OP("scalar", lambda e, p=p, g=g, k=k: e.copy(
                            stgD[:, 1024 * p + 512 * g:1024 * p + 512 * g + 512],
                            pbh[k][:, 0:512]),
                            r=[f"ps{k}"], w=[f"sd{p}_{g}"], tag="dnTev")
                        for i in range(4):
                            icg = 8 * bq + 4 * g + i
                            rb = ((icg // 4) * 4 + c // 4) * 128
                            co = 512 * (icg % 4) + 128 * (c % 4)
                            OP("scalar", lambda e, rb=rb, co=co, p=p, g=g, i=i:
                               e.dma_start(
                                   out=sdnP_d[rb:rb + 128, co:co + 128],
                                   in_=stgD[:, 1024 * p + 512 * g + 128 * i:
                                            1024 * p + 512 * g + 128 * i + 128]),
                               r=[f"sd{p}_{g}"], dma=f"fd{icg % 4}",
                               tag="dn_st")

                dq_tile(wv, nwords, sc_dn[sl][:, 8 * bq:8 * bq + nwords // 16],
                        f"dscf{sl}", "dld", store_dn)
        return unit

    emitD = [mk_dn_unit(c) for c in range(E // 128)]

    na, ngu = len(emitA), len(emitG)
    done_a = la = lg = 0
    for i, gfn in enumerate(emitG):
        while lg <= min(i + 1, ngu - 1):
            loadG[lg]()
            lg += 1
        gfn()
        want = ((i + 1) * na) // ngu
        while la < min(want + 3, na):
            loadA[la]()
            la += 1
        while done_a < want:
            emitA[done_a]()
            done_a += 1
    while done_a < na:
        while la < min(done_a + 4, na):
            loadA[la]()
            la += 1
        emitA[done_a]()
        done_a += 1
    for dfn in emitD:
        dfn()

    sch.barrier()

    wcount = [0]

    def stream_half(srcP, nf, kq, colidx, j):
        i = wcount[0] % 4
        wcount[0] += 1
        half = wb[:, 1024 * i:1024 * i + 1024]
        rb = (kq * nf + colidx) * 128
        eng = "sync" if i % 2 == 0 else "gpsimd"
        OP(eng, lambda e, half=half, rb=rb, j=j, srcP=srcP: e.dma_start(
            out=half, in_=srcP[rb:rb + 128, 1024 * j:1024 * j + 1024]),
            w=[f"wq{i}"], dma=f"wq{i}", tag="wstream")
        return half, f"wq{i}"

    WBALL = ["wq0", "wq1", "wq2", "wq3"]

    def transpose_into_xfm(r, src_bf, srctag="xbf"):
        x5 = src_bf.rearrange("p (b J t h) -> p b h J t", b=2, J=128, t=4, h=2)
        for ec in range(NEC):
            b_, hh, q = ec // 8, (ec % 8) // 4, ec % 4
            k = 6 + (ec % 2)
            OP("tensor", lambda e, b_=b_, hh=hh, q=q, k=k, x5=x5:
               e.transpose(pbh[k][:, 0:128],
                           x5[:, b_, hh, 32 * q:32 * (q + 1), :], ident[:]),
               r=[srctag, "ident"], w=[f"ps{k}"], tag="xT")
            OP("scalar", lambda e, ec=ec, r=r, k=k:
               e.copy(xfm[:, TB * ec + 128 * r:TB * ec + 128 * r + 128],
                      pbh[k][:, 0:128]),
               r=[f"ps{k}"], w=["xfm"], tag="xTev")

    kf = (kf0, kf1)
    vt = (vt0, vt1)

    for blk in range(nblk):
        par = blk % 2
        kfC, kfP = kf[par], kf[1 - par]
        vtC, vtP = vt[par], vt[1 - par]
        kfCt, kfPt = f"kf{par}", f"kf{1 - par}"
        vtCt, vtPt = f"vt{par}", f"vt{1 - par}"
        tb0 = blk * TB

        sch.claim("vector", nop, ["h", "wr0", "wr1", "wr2", "wr3"] + AR1_ATTN,
                  tag="claimA")

        for r in range(NTT):
            Br = B[:, E * r:E * (r + 1)]
            OP("scalar", lambda e, r=r, Br=Br, tb0=tb0: e.dma_start(
                out=Br[:, :], in_=x_d[tb0 + 128 * r:tb0 + 128 * r + 128, :]),
                w=[f"B{r}"], dma=f"xl{r}", tag="xload")
            OP("vector", lambda e, Br=Br: e.tensor_copy(xbf[:], Br[:, :]),
               r=[f"B{r}"], w=["xbf"], tag="xcast")
            transpose_into_xfm(r, xbf)

        if dbg_stop == "p1" and blk == nblk - 1:
            break
        for fg in range(12):
            bks = (0, 1, 2, 3) if fg % 2 == 0 else (4, 5, 6, 7)
            for kq in range(4):
                for j in range(2):
                    qslot, stag = stream_half(sqP_d, 12, kq, fg, j)
                    for kcl in range(2 * j, 2 * j + 2):
                        kc = 4 * kq + kcl
                        slot = qslot[:, 512 * (kcl - 2 * j):512 * (kcl - 2 * j) + 512]
                        for m in range(4):
                            first, last = kc == 0, kc == NEC - 1
                            bk = bks[m]
                            if fg < 8:
                                OP("tensor", lambda e, slot=slot, kc=kc, m=m, bk=bk,
                                   first=first, last=last: e.matmul(
                                       pb[bk][:, :], slot[:, 128 * m:128 * (m + 1)],
                                       xfm[:, TB * kc:TB * (kc + 1)],
                                       start=first, stop=last),
                                   r=[stag, "xfm"], w=[f"ps{bk}"], lazy=(m < 3),
                                   tag="qk_mm")
                            else:
                                OP("tensor", lambda e, slot=slot, kc=kc, m=m, bk=bk,
                                   first=first, last=last: e.matmul(
                                       pb[bk][:, :],
                                       xfm[:, TB * kc + 128 * m:TB * kc + 128 * (m + 1)],
                                       slot[:, :], start=first, stop=last),
                                   r=[stag, "xfm"], w=[f"ps{bk}"], lazy=(m < 3),
                                   tag="v_mm")
            for m in range(4):
                bk = bks[m]
                if fg < 4:
                    hch = 4 * fg + m
                    OP("scalar", lambda e, hch=hch, bk=bk:
                       e.copy(qf[:, TB * hch:TB * (hch + 1)], pb[bk][:, :]),
                       r=[f"ps{bk}"], w=["qf"], tag="q_ev")
                elif fg < 8:
                    hch = 4 * (fg - 4) + m
                    OP("scalar", lambda e, hch=hch, bk=bk, kfC=kfC:
                       e.copy(kfC[:, TB * hch:TB * (hch + 1)], pb[bk][:, :]),
                       r=[f"ps{bk}"], w=[kfCt], tag="k_ev")
                else:
                    vg = fg - 8
                    OP("scalar", lambda e, vg=vg, m=m, bk=bk, vtC=vtC:
                       e.copy(vtC[:, E * m + 512 * vg:E * m + 512 * (vg + 1)],
                              pb[bk][:, :]),
                       r=[f"ps{bk}"], w=[vtCt], tag="v_ev")

        if dbg_stop == "p2" and blk == nblk - 1:
            break
        c_lo = 4 if blk == 0 else 0
        for hh in range(NH):
            pvb = 4 + (hh % 2)
            for half in range(c_lo // 4, 2):
                for cl in range(4):
                    c = 4 * half + cl
                    if c < 4:
                        ksrc, ktag = kfP, kfPt
                        kcol = TB * hh + 128 * c
                    else:
                        ksrc, ktag = kfC, kfCt
                        kcol = TB * hh + 128 * (c - 4)
                    ql = max(0, 128 * (c - 4))
                    qh = min(512, 128 * (c + 1))
                    OP("tensor", lambda e, ksrc=ksrc, kcol=kcol, cl=cl, hh=hh,
                       ql=ql, qh=qh:
                       e.matmul(pb[cl][:, ql:qh], ksrc[:, kcol:kcol + 128],
                                qf[:, TB * hh + ql:TB * hh + qh], start=True,
                                stop=True),
                       r=[ktag, "qf"], w=[f"ps{cl}"], lazy=(cl < 3),
                       tag="score")
                OP("scalar", lambda e: e.activation(
                    es[:, :], pq[:, :], AF.Exp, bias=cc(C_ZERO),
                    scale=cc(C_EXPSC)),
                    r=["ps0", "ps1", "ps2", "ps3", "cst"],
                    w=["esA", "es0", "es1", "es2", "es3"], tag="exp")
                for cl in range(4):
                    c = 4 * half + cl
                    if c < 4:
                        msrc, mt = mkp, "mkp"
                        mcol = 512 - 128 * c
                        vsrc, vtag = vtP, vtPt
                        vcol = E * c + 128 * hh
                    else:
                        msrc, mt = mko, "mko"
                        mcol = 512 - 128 * (c - 4)
                        vsrc, vtag = vtC, vtCt
                        vcol = E * (c - 4) + 128 * hh
                    OP("vector", lambda e, cl=cl, msrc=msrc, mcol=mcol:
                       e.tensor_tensor(es[:, 512 * cl:512 * cl + 512],
                                       es[:, 512 * cl:512 * cl + 512],
                                       msrc[:, mcol:mcol + 512], ALU.mult),
                       r=["esA", mt], w=[f"es{cl}"], tag="mask")
                    OP("tensor", lambda e, cl=cl, c=c, c_lo=c_lo: e.matmul(
                        pb[6][0:1, :], onescol[:, 0:1],
                        es[:, 512 * cl:512 * cl + 512],
                        start=(c == c_lo), stop=(c == 7)),
                        r=[f"es{cl}", "ones"], w=["ps6"], lazy=(c < 7),
                        tag="summ")
                    OP("tensor", lambda e, vsrc=vsrc, vcol=vcol, cl=cl, c=c,
                       c_lo=c_lo, pvb=pvb:
                       e.matmul(pb[pvb][:, :], vsrc[:, vcol:vcol + 128],
                                es[:, 512 * cl:512 * cl + 512],
                                start=(c == c_lo), stop=(c == 7)),
                       r=[vtag, f"es{cl}"], w=[f"ps{pvb}"], lazy=(c < 7),
                       tag="pv")
            OP("scalar", lambda e: e.copy(srow[:, :], pb[6][0:1, :]),
               r=["ps6"], w=["srow"], small=True, tag="sum_ev")
            OP("vector", lambda e: e.tensor_copy(rh[:, :], srow[:, :]),
               r=["srow"], w=["rhi"], small=True, tag="rhi")
            OP("vector", lambda e: e.scalar_tensor_tensor(
                rl[:, :], rh[:, :], -1.0, srow[:, :], ALU.mult, ALU.add),
                r=["rhi", "srow"], w=["rlo"], small=True, tag="rlo")
            OP("tensor", lambda e: e.matmul(pb[7][:, :], onesrow[:, :],
                                            rh[:, :], start=True, stop=False),
               r=["rhi", "ones"], w=["ps7"], lazy=True, tag="bch")
            OP("tensor", lambda e: e.matmul(pb[7][:, :], onesrow[:, :],
                                            rl[:, :], start=False, stop=True),
               r=["rlo", "ones"], w=["ps7"], tag="bcl")
            OP("vector", lambda e: e.reciprocal(recb[:, :], pb[7][:, :]),
               r=["ps7"], w=["rec"], tag="recip")
            OP("vector", lambda e, hh=hh, pvb=pvb: e.scalar_tensor_tensor(
                af[:, TB * hh:TB * (hh + 1)], pb[pvb][:, :], 1.0, recb[:, :],
                ALU.bypass, ALU.mult),
                r=[f"ps{pvb}", "rec"], w=["af"], tag="attn_ev")

        if dbg_stop == "p3" and blk == nblk - 1:
            break
        for og in range(4):
            bks = (0, 1, 2, 3) if og % 2 == 0 else (4, 5, 6, 7)
            for kq in range(4):
                for j in range(2):
                    qslot, stag = stream_half(soP_d, 4, kq, og, j)
                    for kcl in range(2 * j, 2 * j + 2):
                        kc = 4 * kq + kcl
                        slot = qslot[:, 512 * (kcl - 2 * j):512 * (kcl - 2 * j) + 512]
                        for m in range(4):
                            bk = bks[m]
                            OP("tensor", lambda e, slot=slot, kc=kc, m=m, bk=bk:
                               e.matmul(pb[bk][:, :],
                                        af[:, TB * kc + 128 * m:TB * kc + 128 * (m + 1)],
                                        slot[:, :], start=(kc == 0),
                                        stop=(kc == NEC - 1)),
                               r=[stag, "af"], w=[f"ps{bk}"], lazy=(m < 3),
                               tag="pj_mm")
            for r in range(4):
                bk = bks[r]
                OP("vector", lambda e, r=r, og=og, bk=bk: e.scalar_tensor_tensor(
                    A[:, E * r + 512 * og:E * r + 512 * (og + 1)], pb[bk][:, :],
                    1.0, B[:, E * r + 512 * og:E * r + 512 * (og + 1)],
                    ALU.bypass, ALU.add),
                    r=[f"ps{bk}", f"B{r}"], w=[f"A{r}"], tag="resid1")
        if dbg_stop == "p4a" and blk == nblk - 1:
            break
        def mk_ln1(r):
            Ar = A[:, E * r:E * (r + 1)]
            Br = B[:, E * r:E * (r + 1)]
            sc = 6 * r
            cb, cbt = (xbf, "xbf") if r % 2 == 0 else (xbf2, "xb2")

            def st(i):
                return stats[:, sc + i:sc + i + 1]

            def nm(i):
                return f"st{i}_{r}"
            return [
                lambda: OP("vector", lambda e: e.reduce_sum(st(0), Ar[:, :],
                                                            axis=AXX),
                           r=[f"A{r}"], w=[nm(0)], small=True, tag="l1sum"),
                lambda: OP("vector", lambda e: e.tensor_scalar(
                    st(1), st(0), 1.0 / E, None, ALU.mult),
                    r=[nm(0)], w=[nm(1)], small=True, tag="l1mean"),
                lambda: OP("vector", lambda e: e.tensor_scalar(
                    Br[:, :], Ar[:, :], st(1), None, ALU.subtract),
                    r=[f"A{r}", nm(1)], w=[f"B{r}"], tag="l1ctr"),
                lambda: OP("scalar", lambda e: e.activation(
                    xbf[:], Br[:, :], AF.Square, bias=cc(C_ZERO),
                    scale=cc(C_ONE), accum_out=st(2)),
                    r=[f"B{r}", "cst"], w=[nm(2), "xbf"], tag="l1sq"),
                lambda: OP("vector", lambda e: e.tensor_scalar(
                    st(3), st(2), 1.0 / E, LN_EPS, ALU.mult, ALU.add),
                    r=[nm(2)], w=[nm(3)], small=True, tag="l1var"),
                lambda: OP("scalar", lambda e: e.activation(
                    st(4), st(3), AF.Sqrt, bias=cc(C_ZERO), scale=cc(C_ONE)),
                    r=[nm(3), "cst"], w=[nm(4)], small=True, tag="l1std"),
                lambda: OP("vector", lambda e: e.reciprocal(st(5), st(4)),
                           r=[nm(4)], w=[nm(5)], small=True, tag="l1rstd"),
                lambda: OP("vector", lambda e: e.tensor_scalar(
                    Ar[:, :], Br[:, :], st(5), None, ALU.mult),
                    r=[f"B{r}", nm(5)], w=[f"A{r}"], tag="l1app"),
                lambda: (OP("vector", lambda e: e.tensor_copy(cb[:], Ar[:, :]),
                            r=[f"A{r}"], w=[cbt], tag="h1cast"),
                         transpose_into_xfm(r, cb, cbt)),
            ]

        ch1 = [mk_ln1(r) for r in range(NTT)]
        for si in range(len(ch1[0])):
            for r in range(NTT):
                ch1[r][si]()

        if dbg_stop == "p5" and blk == nblk - 1:
            break
        sch.claim("vector", nop, AR1_ATTN + ["h", "wr0", "wr1", "wr2", "wr3"],
                  tag="claimH")

        for g2 in range(NI // 512):
            for side in range(2):
                bks = (0, 1, 2, 3) if side == 0 else (4, 5, 6, 7)
                for kq in range(4):
                    for j in range(2):
                        qslot, stag = stream_half(sguP_d, 22, kq,
                                                  11 * side + g2, j)
                        for kcl in range(2 * j, 2 * j + 2):
                            kc = 4 * kq + kcl
                            slot = qslot[:, 512 * (kcl - 2 * j):512 * (kcl - 2 * j) + 512]
                            for m in range(4):
                                OP("tensor", lambda e, slot=slot, kc=kc, m=m,
                                   bks=bks: e.matmul(
                                       pb[bks[m]][:, :],
                                       slot[:, 128 * m:128 * (m + 1)],
                                       xfm[:, TB * kc:TB * (kc + 1)],
                                       start=(kc == 0), stop=(kc == NEC - 1)),
                                   r=[stag, "xfm"], w=[f"ps{bks[m]}"],
                                   lazy=(m < 3), tag="gu_mm")
            for m in range(4):
                ic = 4 * g2 + m
                gsl = gs[:, 0:512] if m == 0 else xbf[:, 512 * (m - 1):512 * m]
                OP("scalar", lambda e, m=m, gsl=gsl: e.activation(
                    gsl, pb[m][:, :], AF.Silu, bias=cc(C_ZERO),
                    scale=cc(C_ONE)),
                    r=[f"ps{m}", "cst"], w=[f"gsl{m}"], tag="silu")
                OP("vector", lambda e, ic=ic, m=m, gsl=gsl: e.tensor_tensor(
                    h_fm[:, TB * ic:TB * (ic + 1)], gsl, pb[4 + m][:, :],
                    ALU.mult), r=[f"gsl{m}", f"ps{4 + m}"], w=["h"],
                    tag="silmul")

        if dbg_stop == "p6" and blk == nblk - 1:
            break
        for og in range(4):
            bks = (0, 1, 2, 3) if og % 2 == 0 else (4, 5, 6, 7)
            for icq in range(NIC // 4):
                for j in range(2):
                    qslot, stag = stream_half(sdnP_d, 4, icq, og, j)
                    for icl in range(2 * j, 2 * j + 2):
                        ic = 4 * icq + icl
                        slot = qslot[:, 512 * (icl - 2 * j):512 * (icl - 2 * j) + 512]
                        for r in range(4):
                            bk = bks[r]
                            OP("tensor", lambda e, slot=slot, ic=ic, r=r, bk=bk:
                               e.matmul(pb[bk][:, :],
                                        h_fm[:, TB * ic + 128 * r:TB * ic + 128 * (r + 1)],
                                        slot[:, :], start=(ic == 0),
                                        stop=(ic == NIC - 1)),
                               r=[stag, "h"], w=[f"ps{bk}"], lazy=(r < 3),
                               tag="dn_mm")
            for r in range(4):
                bk = bks[r]
                OP("scalar", lambda e, r=r, og=og, bk=bk: e.copy(
                    B[:, E * r + 512 * og:E * r + 512 * (og + 1)], pb[bk][:, :]),
                    r=[f"ps{bk}"], w=[f"B{r}"], tag="y_ev")
        if dbg_stop == "p7a" and blk == nblk - 1:
            break
        def mk_ln2(r, tb0=tb0):
            Ar = A[:, E * r:E * (r + 1)]
            Br = B[:, E * r:E * (r + 1)]
            Wr = wrF[r]
            wn = f"wr{r}"
            dum = ar1[:, 4096 * r:4096 * r + 2048]
            sc = 24 + 10 * r

            def sl(i):
                return stats[:, sc + i:sc + i + 1]

            def sn(i):
                return f"su{i}_{r}"
            return [
                lambda: OP("scalar", lambda e: e.activation(
                    dum, Br[:, :], AF.Square, bias=cc(C_ZERO),
                    scale=cc(C_ONE), accum_out=sl(0)),
                    r=[f"B{r}", "cst"], w=[sn(0), wn], tag="rsq"),
                lambda: OP("vector", lambda e: e.tensor_scalar(
                    sl(1), sl(0), 1.0 / E, RMS_EPS, ALU.mult, ALU.add),
                    r=[sn(0)], w=[sn(1)], small=True, tag="rms"),
                lambda: OP("scalar", lambda e: e.activation(
                    sl(2), sl(1), AF.Sqrt, bias=cc(C_ZERO), scale=cc(C_ONE)),
                    r=[sn(1), "cst"], w=[sn(2)], small=True, tag="rsqt"),
                lambda: OP("vector", lambda e: e.reciprocal(sl(3), sl(2)),
                           r=[sn(2)], w=[sn(3)], small=True, tag="rinv"),
                lambda: OP("vector", lambda e: e.scalar_tensor_tensor(
                    Wr[:, :], Br[:, :], sl(3), Ar[:, :], ALU.mult, ALU.add),
                    r=[f"B{r}", sn(3), f"A{r}"], w=[wn], tag="resid2"),
                lambda: OP("vector", lambda e: e.reduce_sum(
                    sl(4), Wr[:, :], axis=AXX),
                    r=[wn], w=[sn(4)], small=True, tag="l2sum"),
                lambda: OP("vector", lambda e: e.tensor_scalar(
                    sl(5), sl(4), 1.0 / E, None, ALU.mult),
                    r=[sn(4)], w=[sn(5)], small=True, tag="l2mean"),
                lambda: OP("vector", lambda e: e.tensor_scalar(
                    Ar[:, :], Wr[:, :], sl(5), None, ALU.subtract),
                    r=[wn, sn(5)], w=[f"A{r}"], tag="l2ctr"),
                lambda: OP("scalar", lambda e: e.activation(
                    dum, Ar[:, :], AF.Square, bias=cc(C_ZERO),
                    scale=cc(C_ONE), accum_out=sl(6)),
                    r=[f"A{r}", "cst"], w=[sn(6), wn], tag="l2sq"),
                lambda: OP("vector", lambda e: e.tensor_scalar(
                    sl(7), sl(6), 1.0 / E, LN_EPS, ALU.mult, ALU.add),
                    r=[sn(6)], w=[sn(7)], small=True, tag="l2var"),
                lambda: OP("scalar", lambda e: e.activation(
                    sl(8), sl(7), AF.Sqrt, bias=cc(C_ZERO), scale=cc(C_ONE)),
                    r=[sn(7), "cst"], w=[sn(8)], small=True, tag="l2std"),
                lambda: OP("vector", lambda e: e.reciprocal(sl(9), sl(8)),
                           r=[sn(8)], w=[sn(9)], small=True, tag="l2rstd"),
                lambda: OP("vector", lambda e: e.tensor_scalar(
                    Br[:, :], Ar[:, :], sl(9), None, ALU.mult),
                    r=[f"A{r}", sn(9)], w=[f"B{r}"], tag="l2app"),
                lambda: OP("scalar", lambda e: e.dma_start(
                    out=y_d[tb0 + 128 * r:tb0 + 128 * r + 128, :],
                    in_=Br[:, :]),
                    r=[f"B{r}"], dma=f"ys{r}", tag="ystore"),
            ]

        ch2 = [mk_ln2(r) for r in range(NTT)]
        for si in range(len(ch2[0])):
            for r in range(NTT):
                ch2[r][si]()

    sch.barrier()
    tapmap = dict(A=A, B=B, xfm=xfm, qf=qf, es=es, af=af, kf0=kf0, kf1=kf1,
                  vt0=vt0, vt1=vt1, h=ar1, wb=wb, xbf=xbf, stats=stats,
                  sdn=sdnP_d, sgu=sguP_d, sq=sqP_d)
    for (name, ap_out) in tap_outs:
        key = name.split("__")[1]
        sap = tapmap[key]
        n_r = ap_out.shape[0]
        OP("gpsimd", lambda e, ap_out=ap_out, sap=sap, n_r=n_r: e.dma_start(
            out=ap_out, in_=sap[0:n_r, :]), r=[], dma="tap", tag="tap")

    sch.barrier()
    print(f"[build] ops={len(sch.ops)} instrs~={sch.n_instr} "
          f"sbuf_free={nc.sbuf_bytes_remaining} psum_free={nc.psum_banks_remaining}")
    blk_cm = nc.Block()
    block = blk_cm.__enter__()
    sch.emit(block)
    blk_cm.__exit__(None, None, None)
    st.close()
    return nc


def _consts_np():
    ident = np.eye(128, dtype=ml_dtypes.bfloat16)
    onesc = np.ones((128, 1), dtype=ml_dtypes.bfloat16)
    onesr = np.ones((1, 128), dtype=ml_dtypes.bfloat16)
    r = np.arange(128)[:, None]
    u = np.arange(1024)[None, :]
    MP = (r >= (u - 512)).astype(ml_dtypes.bfloat16)
    MO = (r <= (u - 512)).astype(ml_dtypes.bfloat16)
    mask = np.concatenate([MP, MO], axis=1)
    cst = np.tile(np.array(CONSTS, dtype=np.float32), (128, 1))
    return ident, onesc, onesr, mask, cst


def _pi(n):
    out = np.zeros(n, dtype=np.int64)
    b0 = 0
    while b0 < n:
        w = min(1024, n - b0)
        e = np.arange(w)
        j, kn = e // 8, e % 8
        t, hbit = kn // 2, kn % 2
        ep = (w // 2) * hbit + 4 * j + t
        out[b0 + ep] = b0 + e
        b0 += w
    return out


def make_in_map(xb, wqkv, wout, gup, gus, dnp, dns):
    ident, onesc, onesr, mask, cst = _consts_np()
    pi_e = _pi(E)
    pi_i = _pi(NI)
    rows = np.concatenate([pi_i, NI + pi_i])
    return dict(x=np.ascontiguousarray(xb),
                wqkv=np.ascontiguousarray(np.asarray(wqkv)[:, pi_e]),
                wout=np.asarray(wout),
                gup=np.ascontiguousarray(np.asarray(gup)[rows]),
                gus=np.ascontiguousarray(np.asarray(gus)[:, rows]),
                dnp=np.asarray(dnp),
                dns=np.pad(np.asarray(dns), ((0, 4), (0, 0))),
                identin=ident, onescin=onesc,
                onesrin=onesr, maskin=mask, cstin=cst)


def kernel(L0_attn_x, L0_attn_w_qkv, L0_attn_w_out, L0_ffn_gate_up_packed,
           L0_ffn_gate_up_scales, L0_ffn_down_packed, L0_ffn_down_scales):
    nc = build_program(nblk=8)
    in_maps = [make_in_map(L0_attn_x[b], L0_attn_w_qkv, L0_attn_w_out,
                           L0_ffn_gate_up_packed, L0_ffn_gate_up_scales,
                           L0_ffn_down_packed, L0_ffn_down_scales)
               for b in range(8)]
    res = run_bass_kernel_spmd(nc, in_maps, core_ids=list(range(8)))
    return np.stack([np.asarray(r["y"], dtype=np.float32)
                     for r in res.results])
